# version 12
# baseline (speedup 1.0000x reference)
import numpy as np
from contextlib import ExitStack

import concourse.bass as bass
import concourse.mybir as mybir
from concourse.bass_utils import run_bass_kernel_spmd
from concourse.bass import IndirectOffsetOnAxis

F32 = mybir.dt.float32
I32 = mybir.dt.int32
AF = mybir.ActivationFunctionType
ALU = mybir.AluOpType
AX = mybir.AxisListType

B, S, D, H = 4, 4096, 1024, 64
NT = B * S
NCORE = 8
TPC = NT // NCORE
E = 8
HD = 4096
CAP = 4301
CAPP = 4352
NST = CAPP // 128
YROWS = 4360
ZROW = 4352
DUMP0 = E * CAPP
NTBL = E * CAPP + 4096
JSETS = [[0, 3, 5, 6], [1, 2, 4, 7]]
ATT_BOUNDS = [32, 24, 16, 8]
NPAIR = sum(ATT_BOUNDS)
MASKV = -8.0e11
SGW = [512] * 8 + [256]

ENGS = ("sync", "tensor", "vector", "scalar", "gpsimd")


class Ev:
    __slots__ = ("sem", "val", "eng", "seq", "dma")

    def __init__(self, sem, val, eng, seq, dma):
        self.sem, self.val, self.eng, self.seq, self.dma = sem, val, eng, seq, dma


class Prog:
    def __init__(self):
        self.items = {e: [] for e in ENGS}
        self.semcnt = {}
        self.seq = {e: 0 for e in ENGS}
        self.last_drain = {e: -1 for e in ENGS}
        self.sem_names = []

    def sem(self, name):
        self.semcnt[name] = 0
        self.sem_names.append(name)

    def op(self, eng, fn, waits=(), inc=None, inck=1, dma=False):
        il = self.items[eng]
        bysem = {}
        need_drain = False
        for w in waits:
            if w is None:
                continue
            if w.eng == eng and not w.dma:
                if w.seq > self.last_drain[eng]:
                    need_drain = True
            else:
                if bysem.get(w.sem, -1) < w.val:
                    bysem[w.sem] = w.val
        if need_drain:
            self.last_drain[eng] = self.seq[eng] - 1
            il.append(("drain",))
        for sm, v in sorted(bysem.items()):
            il.append(("wait", sm, v))
        incs = []
        ev = None
        if inc is not None:
            self.semcnt[inc] += inck
            incs = [(inc, inck)]
            ev = Ev(inc, self.semcnt[inc], eng, self.seq[eng], dma)
        il.append(("op", fn, incs))
        self.seq[eng] += 1
        return ev


def _owned_blocks(c):
    b, p = divmod(c, 2)
    return [(b, J) for J in JSETS[p]]


def _host_constants(core):
    b, p = divmod(core, 2)
    jset = JSETS[p]
    cst = {}
    cst["ident"] = np.eye(128, dtype=np.float32)
    cst["tri"] = np.triu(np.ones((128, 128), np.float32), 1)
    cst["dmat"] = (np.arange(128, dtype=np.float32)[:, None]
                   - np.arange(512, dtype=np.float32)[None, :])
    dt = np.zeros((128, NPAIR), np.float32)
    pair = 0
    for jj in range(4):
        J = jset[3 - jj]
        for i in range(ATT_BOUNDS[jj]):
            dt[:, pair] = 512.0 * J - 128.0 * i
            pair += 1
    cst["dtbl"] = dt
    gid_of_row = []
    for r in range(NCORE):
        for (bb, J) in _owned_blocks(r):
            gid_of_row.append(bb * 8 + J)
    my_gids = [b * 8 + J for J in jset]
    m32 = np.zeros((32, 4), np.float32)
    for row, g in enumerate(gid_of_row):
        for jj, gme in enumerate(my_gids):
            if g < gme:
                m32[row, jj] = 1.0
    cst["m32"] = m32
    KVLEN = 64 * 2048 + 2048 * 65
    kidx = np.zeros((64, 8), np.int32)
    for kb in range(8):
        pr = 0 if kb in JSETS[0] else 1
        r = 2 * b + pr
        bb = JSETS[pr].index(kb)
        for pp in range(64):
            kidx[pp, kb] = r * KVLEN + pp * 2048 + bb * 512
    cst["kidx"] = kidx
    vidx = np.zeros((128, 32), np.int32)
    for vb in range(32):
        for pp in range(128):
            sv = 128 * vb + pp
            J, u = divmod(sv, 512)
            pr = 0 if J in JSETS[0] else 1
            lt = 512 * JSETS[pr].index(J) + u
            vidx[pp, vb] = (2 * b + pr) * KVLEN + 64 * 2048 + lt * 65
    cst["vidx"] = vidx
    tsl = np.zeros((128, NST), np.int32)
    for j in range(NST):
        tsl[:, j] = core * CAPP + 128 * j + np.arange(128)
    cst["tslice"] = tsl
    tg = np.zeros((128, 32), np.float32)
    for c in range(4):
        g = my_gids[c]
        for tt4 in range(4):
            for k in range(2):
                tg[:, c * 8 + tt4 * 2 + k] = g * 512 + tt4 * 128 + np.arange(128)
    cst["tglob"] = tg
    return cst


def _shard_inputs(inputs):
    x = np.asarray(inputs["L0_attn_x"], dtype=np.float32)
    xf = np.ascontiguousarray(x.reshape(NT, D))
    wq = np.asarray(inputs["L0_attn_wq"], np.float32)
    wk = np.asarray(inputs["L0_attn_wk"], np.float32)
    wv = np.asarray(inputs["L0_attn_wv"], np.float32)
    wg = np.asarray(inputs["L0_ffn_Wg"], np.float32)
    wqk = np.ascontiguousarray(np.concatenate([wq, wk], axis=1))
    wvg = np.ascontiguousarray(np.concatenate([wv, wg], axis=1))
    bqk = np.ascontiguousarray(np.concatenate(
        [np.asarray(inputs["L0_attn_bq"], np.float32),
         np.asarray(inputs["L0_attn_bk"], np.float32)]).reshape(128, 1))
    bvg = np.ascontiguousarray(np.concatenate(
        [np.asarray(inputs["L0_attn_bv"], np.float32),
         np.zeros(8, np.float32)]).reshape(72, 1))
    wo = np.ascontiguousarray(np.asarray(inputs["L0_attn_wo"], np.float32))
    bo = np.ascontiguousarray(
        np.asarray(inputs["L0_attn_bo"], np.float32).reshape(1, D))
    W1 = np.asarray(inputs["L0_ffn_W1"], np.float32)
    W2 = np.asarray(inputs["L0_ffn_W2"], np.float32)
    b1 = np.asarray(inputs["L0_ffn_b1"], np.float32)
    b2 = np.asarray(inputs["L0_ffn_b2"], np.float32)
    in_maps = []
    for c in range(NCORE):
        cst = _host_constants(c)
        rows = [xf[bb * S + J * 512: bb * S + (J + 1) * 512]
                for (bb, J) in _owned_blocks(c)]
        im = dict(
            x_own=np.ascontiguousarray(np.concatenate(rows, axis=0)),
            x_full=xf,
            wqk=wqk, wvg=wvg, bqk=bqk, bvg=bvg, wo=wo, bo_row=bo,
            w1=np.ascontiguousarray(W1[c]), w2=np.ascontiguousarray(W2[c]),
            b1c=np.ascontiguousarray(b1[c].reshape(32, 128).T),
            b2row=np.ascontiguousarray(b2[c].reshape(1, D)),
            **cst)
        in_maps.append(im)
    return in_maps


def _unshard_output(results):
    out = np.empty((NT, D), np.float32)
    for c in range(NCORE):
        y = np.asarray(results[c]["y_out"])
        for j, (bb, J) in enumerate(_owned_blocks(c)):
            out[bb * S + J * 512: bb * S + (J + 1) * 512] = y[j * 512:(j + 1) * 512]
    return out.reshape(B, S, D)


def build_program(stop_after=None, debug_taps=()):
    nc = bass.Bass("TRN2", target_bir_lowering=False, debug=False)
    P = Prog()

    DI = {}
    for name, shape, dt in [
        ("x_own", [TPC, D], F32), ("x_full", [NT, D], F32),
        ("wqk", [D, 128], F32), ("wvg", [D, 72], F32),
        ("bqk", [128, 1], F32), ("bvg", [72, 1], F32),
        ("wo", [64, D], F32), ("bo_row", [1, D], F32),
        ("w1", [D, HD], F32), ("w2", [HD, D], F32),
        ("b1c", [128, 32], F32), ("b2row", [1, D], F32),
        ("ident", [128, 128], F32), ("tri", [128, 128], F32),
        ("dmat", [128, 512], F32), ("dtbl", [128, NPAIR], F32),
        ("m32", [32, 4], F32), ("kidx", [64, 8], I32),
        ("vidx", [128, 32], I32), ("tslice", [128, NST], I32),
        ("tglob", [128, 32], F32),
    ]:
        DI[name] = nc.dram_tensor(name, shape, dt, kind="ExternalInput").ap()
    y_out = nc.dram_tensor("y_out", [TPC, D], F32, kind="ExternalOutput").ap()
    DBG = {}
    for (dname, sname, shape) in debug_taps:
        DBG[dname] = nc.dram_tensor(dname, shape, F32, kind="ExternalOutput").ap()

    cc_cnt_in = nc.dram_tensor("cc_cnt_in", [1, 32], F32)
    cc_cnt_out = nc.dram_tensor("cc_cnt_out", [8, 32], F32, addr_space="Shared")
    bgl_scr = nc.dram_tensor("bgl_scr", [1, 32], F32)
    KVLEN = 64 * 2048 + 2048 * 65
    kv_in = nc.dram_tensor("kv_in", [KVLEN], F32)
    kv_all = nc.dram_tensor("kv_all", [8 * KVLEN], F32, addr_space="Shared")
    pscr = nc.dram_tensor("pscr", [TPC, D], F32)
    w1s = nc.dram_tensor("w1s", [128, 65536], mybir.dt.bfloat16)
    w2s = nc.dram_tensor("w2s", [128, 65536], mybir.dt.bfloat16)
    tbl_in = nc.dram_tensor("tbl_in", [NTBL, 4], F32)
    tbl_out = nc.dram_tensor("tbl_out", [NTBL, 4], F32, addr_space="Shared")
    y_in = nc.dram_tensor("y_in", [YROWS, D], F32)
    y_all = nc.dram_tensor("y_all", [8 * YROWS, D], F32, addr_space="Shared")
    DRAMT = {"y_in": y_in, "y_all": y_all, "tbl_out": tbl_out}

    es = ExitStack()
    with es:
        SBT = {}
        _cursor = [20480]

        SBOFF = {}

        def SB(n, s, dt=F32, at=None):
            nbytes = int(np.prod(s[1:])) * mybir.dt.size(dt) if len(s) > 1 else 4
            if at is None:
                off = _cursor[0]
                _cursor[0] = (off + nbytes + 63) // 64 * 64
            else:
                off = at
            SBOFF[n] = off
            SBT[n] = nc.alloc_sbuf_tensor_at(n, s, dt, offset=off)
            return SBT[n]

        def PSU(n, s):
            SBT[n] = es.enter_context(nc.psum_tensor(n, s, F32))
            return SBT[n]

        SBT.update(DRAMT)
        s_xt = [SB("s_xt0", [128, D]), SB("s_xt1", [128, D])]
        s_sum = SB("s_sum", [128, 16]); s_ssq = SB("s_ssq", [128, 16])
        s_m = SB("s_m", [128, 16]); s_var = SB("s_var", [128, 16])
        s_std = SB("s_std", [128, 16]); s_c1 = SB("s_c1", [128, 16])
        s_rmat = SB("s_rmat", [128, 16]); s_mrmat = SB("s_mrmat", [128, 16])
        s_xng = SB("s_xng", [128, 4 * D])
        s_xfm = SB("s_xfm", [128, 8 * 512])
        s_wqk = SB("s_wqk", [128, 8 * 128]); s_wvg = SB("s_wvg", [128, 8 * 72])
        s_bqk = SB("s_bqk", [128, 1]); s_bvg = SB("s_bvg", [72, 1])
        s_wo = SB("s_wo", [64, D])
        s_b1c = SB("s_b1c", [128, 32])
        s_b2bc = SB("s_b2bc", [128, D]); s_bobc = SB("s_bobc", [128, D])
        s_id = SB("s_id", [128, 128]); s_tri = SB("s_tri", [128, 128])
        s_dmat = SB("s_dmat", [128, 512]); s_dtbl = SB("s_dtbl", [128, NPAIR])
        s_m32 = SB("s_m32", [32, 4])
        s_one1 = SB("s_one1", [1, 128]); s_onecol = SB("s_onecol", [128, 1])
        s_iota8 = SB("s_iota8", [128, 8]); s_io64 = SB("s_io64", [128, 8])
        s_slotid = SB("s_slotid", [128, 32])
        s_kidx = SB("s_kidx", [128, 8], I32); s_vidx = SB("s_vidx", [128, 32], I32)
        s_tslice = SB("s_tslice", [128, NST], I32)
        s_tglob = SB("s_tglob", [128, 32])
        s_z = SB("s_z", [65, TPC])
        s_recip = SB("s_recip", [128, 16])
        s_vt = [SB("s_vt0", [128, 65]), SB("s_vt1", [128, 65])]
        s_logit = SB("s_logit", [128, 16 * 8])
        s_m8 = SB("s_m8", [128, 16 * 8])
        s_m1 = SB("s_m1", [128, 16]); s_m2 = SB("s_m2", [128, 16])
        s_m1n = SB("s_m1n", [128, 16]); s_m2n = SB("s_m2n", [128, 16])
        s_e1 = SB("s_e1", [128, 16]); s_e2 = SB("s_e2", [128, 16])
        s_g1 = SB("s_g1", [128, 16]); s_g2 = SB("s_g2", [128, 16])
        s_eq = SB("s_eq", [128, 8]); s_ta = SB("s_ta", [128, 8])
        s_le = SB("s_le", [128, 8]); s_excl = SB("s_excl", [128, 1])
        s_M = SB("s_M", [128, 256])
        s_R2 = SB("s_R2", [128, 128])
        s_trip = SB("s_trip", [128, 128])
        s_tot = SB("s_tot", [1, 128]); s_tots = SB("s_tots", [1, 128])
        s_tinc = SB("s_tinc", [1, 128])
        s_pos = SB("s_pos", [128, 256])
        s_cnt = SB("s_cnt", [1, 32])
        s_ccag = SB("s_ccag", [32, 8])
        s_bgl = SB("s_bgl", [4, 8]); s_bglr = SB("s_bglr", [1, 32])
        s_bglb = SB("s_bglb", [128, 32])
        s_esel = SB("s_esel", [128, 32]); s_psel = SB("s_psel", [128, 32])
        s_scr32 = SB("s_scr32", [128, 32])
        s_valid = SB("s_valid", [128, 32])
        s_sa = SB("s_sa", [128, 32]); s_sb = SB("s_sb", [128, 32])
        s_gsel = SB("s_gsel", [128, 32])
        s_dest = SB("s_dest", [128, 32], I32)
        s_yidxf = SB("s_yidxf", [128, 32])
        s_yidx = SB("s_yidx", [128, 32], I32)
        s_payl = SB("s_payl", [128, 128])
        s_zero = SB("s_zero", [128, 512])
        s_cst = SB("s_cst", [128, 2])
        nc.const_aps.register(F32, 1e-5, s_cst[:, 0:1])
        nc.const_aps.register(F32, 0.125, s_cst[:, 1:2])
        s_trow = SB("s_trow", [128, NST * 4])
        s_tcast = SB("s_tcast", [128, NST], I32)
        s_xg = [SB(f"s_xg{i}", [128, D]) for i in range(4)]
        s_w1hl = SB("s_w1hl", [128, 2 * 4096], mybir.dt.bfloat16)
        s_w2hl = SB("s_w2hl", [128, 2 * 1024], mybir.dt.bfloat16)
        s_yev = [SB("s_yev0", [128, 512]), SB("s_yev1", [128, 512])]

        s_zo = SB("s_zo", [128, D])
        s_xb = SB("s_xb", [128, D])
        s_spf = [SB("s_spf0", [128, 512]), SB("s_spf1", [128, 512])]
        s_sph = [SB("s_sph0", [128, 512], mybir.dt.bfloat16),
                 SB("s_sph1", [128, 512], mybir.dt.bfloat16)]
        s_spl = [SB("s_spl0", [128, 512], mybir.dt.bfloat16),
                 SB("s_spl1", [128, 512], mybir.dt.bfloat16)]
        AB = (_cursor[0] + 63) // 64 * 64
        s_kallF = SB("s_kallF", [128, S], at=AB + 0)
        s_kall = s_kallF
        s_vall = SB("s_vall", [128, 32 * 65], at=AB + 16384)
        s_mask = SB("s_mask", [128, 512], at=AB + 24704)
        s_u = [SB(f"s_u{i}", [128, 512], at=AB + 26752 + 2048 * i)
               for i in range(4)]
        s_pt = [SB(f"s_pt{i}", [128, 512], at=AB + 34944 + 2048 * i)
                for i in range(4)]
        s_qF = SB("s_qF", [128, TPC], at=AB + 43136)
        s_q = s_qF
        s_sq = SB("s_sq", [128, D], at=AB + 0)
        s_kev = SB("s_kev", [64, 2 * 512], at=AB + 55424)
        s_vgev = SB("s_vgev", [72, 2 * 512], at=AB + 59520)
        s_Mp = SB("s_Mp", [128, 256], at=AB + 63616)
        s_pexc = SB("s_pexc", [128, 128], at=AB + 64640)
        s_hh = SB("s_hh", [128, 32 * 512], mybir.dt.bfloat16, at=AB + 0)
        s_hl = SB("s_hl", [128, 32 * 512], mybir.dt.bfloat16, at=AB + 32768)
        s_xgfh = SB("s_xgfh", [128, 8 * 512], mybir.dt.bfloat16,
                    at=SBOFF["s_xfm"])
        s_xgfl = SB("s_xgfl", [128, 8 * 512], mybir.dt.bfloat16,
                    at=SBOFF["s_xfm"] + 8192)
        s_gel = SB("s_gel", [128, 512], at=SBOFF["s_zero"])
        s_yg0 = [SB("s_yg0a", [128, D], at=AB + 0),
                 SB("s_yg0b", [128, D], at=AB + 24576)]
        s_yg1 = [SB("s_yg1a", [128, D], at=AB + 4096),
                 SB("s_yg1b", [128, D], at=AB + 28672)]

        s_acc = SB("s_acc", [128, D], at=AB + 16384)
        s_out = SB("s_out", [128, D], at=AB + 20480)
        assert AB + 65536 <= 229376, f"SBUF over budget: {AB + 65536}"

        pb = [PSU(f"pb{i}", [128, 512]) for i in range(8)]

        for nm in ["dmaA", "dmaB", "ld", "ld2", "tp", "fm", "mm", "vr", "sr",
                   "kvkA", "kvkB", "kvvA", "kvvB",
                   "cck", "ccv", "ccc", "cct", "ccy0", "ccy1", "ccy2", "ccy3", "ccy4",
                   "gz1", "gz2", "gk", "gv", "gt", "gs", "gc",
                   "gx0", "gx1", "gx2", "gx3", "gq",
                   "gy0a", "gy0b", "gy1a", "gy1b",
                   "w1a", "w1b", "w2a", "w2b", "yya", "yyb", "oo",
                   "sl0", "sl1", "sh0", "sh1"]:
            P.sem(nm)

        EV = {}

        def ld(dst, src):
            return P.op("sync", lambda e, d=dst, s=src: e.dma_start(out=d, in_=s),
                        inc="ld", inck=16, dma=True)

        for dcc in range(8):
            EV[f"wqk{dcc}"] = ld(s_wqk[:, dcc * 128:(dcc + 1) * 128],
                                 DI["wqk"][dcc * 128:(dcc + 1) * 128, :])
            EV[f"wvg{dcc}"] = ld(s_wvg[:, dcc * 72:(dcc + 1) * 72],
                                 DI["wvg"][dcc * 128:(dcc + 1) * 128, :])
        EV["kidx"] = ld(s_kidx[0:64, :], DI["kidx"][:, :])
        EV["kidx_b"] = ld(s_kidx[64:128, :], DI["kidx"][:, :])
        for n1, t1 in [("bqk", s_bqk), ("bvg", s_bvg), ("wo", s_wo),
                       ("b1c", s_b1c), ("ident", s_id), ("tri", s_tri),
                       ("dmat", s_dmat), ("dtbl", s_dtbl), ("m32", s_m32),
                       ("vidx", s_vidx), ("tslice", s_tslice),
                       ("tglob", s_tglob)]:
            EV[n1] = ld(t1[:], DI[n1][:, :])
        EV["b2row"] = ld(s_b2bc[0:1, :], DI["b2row"][:, :])
        EV["bo_row"] = ld(s_bobc[0:1, :], DI["bo_row"][:, :])
        _ld_last = EV["bo_row"]
        for _k in list(EV.keys()):
            EV[_k] = _ld_last

        ev_io = P.op("gpsimd", lambda e: e.iota(
            s_iota8[:], pattern=[[1, 8]], base=0, channel_multiplier=0,
            allow_small_or_imprecise_dtypes=True), inc="gc")
        ev_sid = P.op("gpsimd", lambda e: e.iota(
            s_slotid[:], pattern=[[1, 32]], base=0, channel_multiplier=32,
            allow_small_or_imprecise_dtypes=True), inc="gc")
        ev_o1 = P.op("gpsimd", lambda e: e.memset(s_one1[:], 1.0), inc="gc")
        ev_oc = P.op("gpsimd", lambda e: e.memset(s_onecol[:], 1.0), inc="gc")
        ev_zz = P.op("gpsimd", lambda e: e.memset(s_zero[:], 0.0), inc="gc")
        ev_ceps = P.op("gpsimd", lambda e: e.memset(s_cst[:, 0:1], 1e-5), inc="gc")
        ev_c125 = P.op("gpsimd", lambda e: e.memset(s_cst[:, 1:2], 0.125), inc="gc")
        ev_i64 = P.op("vector", lambda e: e.tensor_scalar(
            s_io64[:], s_iota8[:], -64.0, None, ALU.add), waits=[ev_io], inc="vr")

        tz = tbl_in.ap().rearrange("(a b) c -> a (b c)", a=128)
        ev_tz = []
        for j in range(3):
            w = min(512, NTBL * 4 // 128 - j * 512)
            ev_tz.append(P.op("gpsimd", lambda e, j=j, w=w: e.dma_start(
                out=tz[:, j * 512:j * 512 + w], in_=s_zero[:, 0:w]),
                waits=[ev_zz], inc="gz1", inck=16, dma=True))
        ev_yz = []
        for half in range(2):
            ev_yz.append(P.op("gpsimd", lambda e, h=half: e.dma_start(
                out=y_in.ap()[ZROW:YROWS, h * 512:(h + 1) * 512],
                in_=s_zero[0:YROWS - ZROW, 0:512]),
                waits=[ev_zz], inc="gz1", inck=16, dma=True))

        prev_bc = None
        for (dst, src_ev) in [(s_b2bc, EV["b2row"]), (s_bobc, EV["bo_row"])]:
            for dgi in range(2):
                evm = P.op("tensor", lambda e, d=dst, g=dgi: e.matmul(
                    pb[6][:, 0:512], s_one1[:], d[0:1, g * 512:(g + 1) * 512],
                    start=True, stop=True),
                    waits=[src_ev, ev_o1, prev_bc], inc="mm")
                prev_bc = P.op("vector", lambda e, d=dst, g=dgi: e.tensor_copy(
                    d[:, g * 512:(g + 1) * 512], pb[6][:, 0:512]),
                    waits=[evm], inc="vr")
        ev_bcast_done = prev_bc

        split_state = {"ld": [None, None], "cast": [None, None],
                       "st": [None, None], "i": 0}
        SPLIT_JOBS = []
        SPLIT_EV = {}
        for c0 in range(0, HD, 512):
            for r0 in range(0, D, 128):
                SPLIT_JOBS.append((0, DI["w1"], r0, c0))
        for r0 in range(0, HD, 128):
            for c0 in range(0, D, 512):
                SPLIT_JOBS.append((1, DI["w2"], r0, c0))

        def _split_dst(wi, r0, c0, hilo):
            if wi == 0:
                v = w1s.ap().rearrange("p (d g h c) -> p d g h c",
                                       d=8, g=16, h=2, c=256)
                dcc, g0 = r0 // 128, c0 // 256
                return v[:, dcc:dcc + 1, g0:g0 + 2, hilo:hilo + 1, :]
            v = w2s.ap().rearrange("p (t g h c) -> p t g h c",
                                   t=32, g=2, h=2, c=512)
            return v[:, r0 // 128:r0 // 128 + 1, c0 // 512:c0 // 512 + 1,
                     hilo:hilo + 1, :]

        def emit_split(n):
            st = split_state
            for _ in range(n):
                i = st["i"]
                if i >= len(SPLIT_JOBS):
                    return
                st["i"] += 1
                wi, wsrc, r0, c0 = SPLIT_JOBS[i]
                b = i % 2
                evl = P.op("scalar", lambda e, b=b, wsrc=wsrc, r0=r0, c0=c0:
                           e.dma_start(out=s_spf[b][:],
                                       in_=wsrc[r0:r0 + 128, c0:c0 + 512]),
                           waits=[st["cast"][b], st["ld"][b]],
                           inc=f"sl{b}", inck=16, dma=True)
                st["ld"][b] = evl
                evh = P.op("vector", lambda e, b=b: e.tensor_copy(
                    s_sph[b][:], s_spf[b][:]),
                    waits=[evl, st["st"][b]], inc="vr")
                evc = P.op("vector", lambda e, b=b: e.tensor_tensor(
                    s_spl[b][:], s_spf[b][:], s_sph[b][:], ALU.subtract),
                    waits=[evh], inc="vr")
                st["cast"][b] = evc
                evs1 = P.op("gpsimd", lambda e, b=b, wi=wi, r0=r0, c0=c0:
                            e.dma_start(out=_split_dst(wi, r0, c0, 0),
                                        in_=s_sph[b][:]),
                            waits=[evc, st["st"][b]],
                            inc=f"sh{b}", inck=16, dma=True)
                evs2 = P.op("gpsimd", lambda e, b=b, wi=wi, r0=r0, c0=c0:
                            e.dma_start(out=_split_dst(wi, r0, c0, 1),
                                        in_=s_spl[b][:]),
                            waits=[evc, evs1], inc=f"sh{b}", inck=16, dma=True)
                st["st"][b] = evs2
                SPLIT_EV[(wi, r0, c0)] = evs2
            return

        def split_done_ev():
            return [split_state["st"][0], split_state["st"][1]]

        ev_xn = [None] * 16
        ev_sq_prev = [None]
        ev_xload_prev = [None, None]
        ev_tile_tp = [None] * 16
        ev_logit = [None] * 16
        ev_r_all = [None] * 16
        ev_mr_all = [None] * 16
        ev_kdma = []
        ev_vdma = [None] * 16
        ev_qev_all = [None] * 4
        ev_qb = []
        prev_fm = {}
        ev_qk_free = None
        ev_vg_free = None
        ev_vge_used = [None, None]
        ev_kev_used = [None, None]

        for g5 in range(4):
            g4 = slice(g5 * 4, g5 * 4 + 4)
            ev_red = []
            for ti in range(4):
                tt = g5 * 4 + ti
                evx = P.op("sync", lambda e, tt=tt: e.dma_start(
                    out=s_xng[:, (tt % 4) * D:(tt % 4 + 1) * D],
                    in_=DI["x_own"][tt * 128:(tt + 1) * 128, :]),
                    waits=[ev_tile_tp[tt - 4] if tt >= 4 else None,
                           ev_xload_prev[tt % 2]],
                    inc="dmaA" if tt % 2 == 0 else "dmaB",
                    inck=16, dma=True)
                ev_xload_prev[tt % 2] = evx
                ev_red.append(P.op("vector", lambda e, tt=tt: e.tensor_reduce(
                    s_sum[:, tt:tt + 1], s_xng[:, (tt % 4) * D:(tt % 4 + 1) * D],
                    AX.X, ALU.add), waits=[evx], inc="vr"))
                ev_sq_prev[0] = P.op("scalar", lambda e, tt=tt: e.activation(
                    out=s_sq[:], in_=s_xng[:, (tt % 4) * D:(tt % 4 + 1) * D],
                    func=AF.Square, bias=0.0, scale=1.0,
                    accum_out=s_ssq[:, tt:tt + 1]),
                    waits=[evx, ev_sq_prev[0]], inc="sr")
                ev_red.append(ev_sq_prev[0])
                emit_split(2)
            ev_m = P.op("vector", lambda e, g4=g4: e.tensor_scalar(
                s_m[:, g4], s_sum[:, g4], 1.0 / D, None, ALU.mult),
                waits=ev_red, inc="vr")
            ev_msq = P.op("vector", lambda e, g4=g4: e.tensor_tensor(
                s_c1[:, g4], s_m[:, g4], s_m[:, g4], ALU.mult),
                waits=[ev_m], inc="vr")
            ev_ms = P.op("vector", lambda e, g4=g4: e.tensor_scalar(
                s_var[:, g4], s_ssq[:, g4], 1.0 / D, None, ALU.mult),
                waits=ev_red, inc="vr")
            ev_var = P.op("vector", lambda e, g4=g4: e.tensor_tensor(
                s_var[:, g4], s_var[:, g4], s_c1[:, g4], ALU.subtract),
                waits=[ev_msq, ev_ms], inc="vr")
            ev_std = P.op("scalar", lambda e, g4=g4: e.activation(
                out=s_std[:, g4], in_=s_var[:, g4], func=AF.Sqrt,
                bias=1e-5, scale=1.0), waits=[ev_var, ev_ceps], inc="sr")
            ev_r = P.op("vector", lambda e, g4=g4: e.reciprocal(
                s_rmat[:, g4], s_std[:, g4]), waits=[ev_std], inc="vr")
            ev_mr0 = P.op("vector", lambda e, g4=g4: e.tensor_tensor(
                s_c1[:, g4], s_m[:, g4], s_rmat[:, g4], ALU.mult),
                waits=[ev_r], inc="vr")
            ev_mr = P.op("vector", lambda e, g4=g4: e.tensor_scalar(
                s_mrmat[:, g4], s_c1[:, g4], -1.0, None, ALU.mult),
                waits=[ev_mr0], inc="vr")
            for ti in range(4):
                tt = g5 * 4 + ti
                ev_r_all[tt] = ev_r
                ev_mr_all[tt] = ev_mr
                ev_xn[tt] = P.op("scalar", lambda e, tt=tt: e.activation(
                    out=s_xng[:, (tt % 4) * D:(tt % 4 + 1) * D],
                    in_=s_xng[:, (tt % 4) * D:(tt % 4 + 1) * D],
                    func=AF.Identity, bias=s_mrmat[:, tt:tt + 1],
                    scale=s_rmat[:, tt:tt + 1]),
                    waits=[ev_mr], inc="sr")

            evfm_all = []
            for ti in range(4):
                tt = g5 * 4 + ti
                for dcc in range(8):
                    bank = (ti * 8 + dcc) % 2
                    evt = P.op("tensor", lambda e, tt=tt, dcc=dcc, bank=bank:
                               e.transpose(
                        pb[bank][:, 0:128],
                        s_xng[:, (tt % 4) * D + dcc * 128:(tt % 4) * D + (dcc + 1) * 128],
                        s_id[:]),
                        waits=[ev_xn[tt], EV["ident"], prev_fm.get(bank)],
                        inc="tp")
                    prev_fm[bank] = P.op(
                        "vector", lambda e, ti=ti, dcc=dcc, bank=bank: e.tensor_copy(
                            s_xfm[:, dcc * 512 + ti * 128: dcc * 512 + (ti + 1) * 128],
                            pb[bank][:, 0:128]), waits=[evt], inc="fm")
                    evfm_all.append(prev_fm[bank])
                ev_tile_tp[tt] = evfm_all[-1]
            last = None
            for dcc in range(8):
                last = P.op("tensor", lambda e, dcc=dcc: e.matmul(
                    pb[2][:, 0:512], s_wqk[:, dcc * 128:(dcc + 1) * 128],
                    s_xfm[:, dcc * 512:(dcc + 1) * 512],
                    start=(dcc == 0), stop=(dcc == 7)),
                    waits=(evfm_all + [EV[f"wqk{dcc}"], ev_qk_free]
                           if dcc == 0 else [EV[f"wqk{dcc}"]]),
                    inc="mm" if dcc == 7 else None)
            ev_qkmm = last
            for dcc in range(8):
                last = P.op("tensor", lambda e, dcc=dcc: e.matmul(
                    pb[3][0:72, 0:512], s_wvg[:, dcc * 72:(dcc + 1) * 72],
                    s_xfm[:, dcc * 512:(dcc + 1) * 512],
                    start=(dcc == 0), stop=(dcc == 7)),
                    waits=([EV[f"wvg{dcc}"], ev_vg_free] if dcc == 0
                           else [EV[f"wvg{dcc}"]]),
                    inc="mm" if dcc == 7 else None)
            ev_vgmm = last
            ev_qev = P.op("scalar", lambda e, g5=g5: e.activation(
                out=s_q[0:64, g5 * 512:(g5 + 1) * 512], in_=pb[2][0:64, 0:512],
                func=AF.Identity, bias=s_bqk[0:64, 0:1], scale=1.0),
                waits=[ev_qkmm, EV["bqk"]], inc="sr")
            ev_qev_all[g5] = ev_qev
            ev_kev = P.op("scalar", lambda e, g5=g5: e.activation(
                out=s_kev[:, (g5 % 2) * 512:(g5 % 2 + 1) * 512],
                in_=pb[2][64:128, 0:512], func=AF.Identity,
                bias=s_bqk[64:128, 0:1], scale=1.0),
                waits=[ev_qkmm, ev_kev_used[g5 % 2]], inc="sr")
            ev_qk_free = ev_kev
            ev_vge = P.op("scalar", lambda e, g5=g5: e.activation(
                out=s_vgev[:, (g5 % 2) * 512:(g5 % 2 + 1) * 512],
                in_=pb[3][0:72, 0:512], func=AF.Identity, bias=s_bvg[:, 0:1],
                scale=1.0), waits=[ev_vgmm, EV["bvg"], ev_vge_used[g5 % 2]],
                inc="sr")
            ev_vg_free = ev_vge
            kvkv = kv_in.ap()[0:64 * 2048].rearrange("(h w) -> h w", h=64)
            ev_kd = P.op("scalar", lambda e, g5=g5, kvkv=kvkv: e.dma_start(
                out=kvkv[:, g5 * 512:(g5 + 1) * 512],
                in_=s_kev[:, (g5 % 2) * 512:(g5 % 2 + 1) * 512]),
                waits=[ev_kev], inc="kvkA" if g5 % 2 == 0 else "kvkB",
                inck=16, dma=True)
            ev_kdma.append(ev_kd)
            ev_kev_used[g5 % 2] = ev_kd
            last_vt = None
            for ti in range(4):
                tt = g5 * 4 + ti
                evtp = P.op("tensor", lambda e, g5=g5, ti=ti: e.transpose(
                    pb[4][0:128, 0:72],
                    s_vgev[:, (g5 % 2) * 512 + ti * 128:(g5 % 2) * 512 + (ti + 1) * 128],
                    s_id[0:72, 0:72]),
                    waits=[ev_vge, last_vt], inc="tp")
                evc1 = P.op("vector", lambda e, tt=tt: e.tensor_copy(
                    s_vt[tt % 2][:, 0:64], pb[4][:, 0:64]),
                    waits=[evtp, ev_vdma[tt - 2] if tt >= 2 else None], inc="fm")
                evc2 = P.op("vector", lambda e, tt=tt: e.memset(
                    s_vt[tt % 2][:, 64:65], 1.0),
                    waits=[ev_vdma[tt - 2] if tt >= 2 else None], inc="fm")
                evc3 = P.op("vector", lambda e, tt=tt: e.tensor_copy(
                    s_logit[:, tt * 8:(tt + 1) * 8], pb[4][:, 64:72]),
                    waits=[evtp], inc="fm")
                ev_logit[tt] = evc3
                last_vt = evc3
                kvvv = kv_in.ap()[64 * 2048:KVLEN].rearrange("(a w) -> a w", w=65)
                ev_vdma[tt] = P.op("gpsimd", lambda e, tt=tt, kvvv=kvvv: e.dma_start(
                    out=kvvv[tt * 128:(tt + 1) * 128, :], in_=s_vt[tt % 2][:]),
                    waits=[evc1, evc2], inc="kvvA" if tt % 2 == 0 else "kvvB",
                    inck=16, dma=True)
            ev_vge_used[g5 % 2] = last_vt
            ev_qb.append(P.op("gpsimd", lambda e, g5=g5: e.dma_start(
                out=s_qF[64:128, g5 * 512:(g5 + 1) * 512],
                in_=s_qF[0:64, g5 * 512:(g5 + 1) * 512]),
                waits=[ev_qev], inc="gq", inck=16, dma=True))

        ev_agkv = P.op("gpsimd", lambda e: e.collective_compute(
            "AllGather", ALU.bypass, replica_groups=[list(range(8))],
            ins=[kv_in.ap().opt()], outs=[kv_all.ap().opt()]),
            waits=ev_kdma + list(ev_vdma), inc="cck", dma=True)

        if stop_after == "phaseA":
            _finish(P, DBG, debug_taps, SBT, y_out, [ev_agkv])
            _emit(P, nc, es)
            return nc

        ev_top = [None] * 16
        for tt in range(16):
            evmax = P.op("vector", lambda e, tt=tt: e.max(
                s_m8[:, tt * 8:(tt + 1) * 8], s_logit[:, tt * 8:(tt + 1) * 8]),
                waits=[ev_logit[tt]], inc="vr")
            ev1 = P.op("vector", lambda e, tt=tt: e.tensor_copy(
                s_m1[:, tt:tt + 1], s_m8[:, tt * 8:tt * 8 + 1]),
                waits=[evmax], inc="vr")
            ev2 = P.op("vector", lambda e, tt=tt: e.tensor_copy(
                s_m2[:, tt:tt + 1], s_m8[:, tt * 8 + 1:tt * 8 + 2]),
                waits=[evmax], inc="vr")
            eveq = P.op("vector", lambda e, tt=tt: e.tensor_scalar(
                s_eq[:], s_logit[:, tt * 8:(tt + 1) * 8], s_m1[:, tt:tt + 1],
                None, ALU.is_equal), waits=[ev1], inc="vr")
            evta = P.op("vector", lambda e: e.tensor_tensor(
                s_ta[:], s_eq[:], s_io64[:], ALU.mult),
                waits=[eveq, ev_i64], inc="vr")
            evtb = P.op("vector", lambda e: e.tensor_scalar(
                s_ta[:], s_ta[:], 64.0, None, ALU.add), waits=[evta], inc="vr")
            eve1 = P.op("vector", lambda e, tt=tt: e.tensor_reduce(
                s_e1[:, tt:tt + 1], s_ta[:], AX.X, ALU.min),
                waits=[evtb], inc="vr")
            eveq2 = P.op("vector", lambda e, tt=tt: e.tensor_scalar(
                s_eq[:], s_logit[:, tt * 8:(tt + 1) * 8], s_m2[:, tt:tt + 1],
                None, ALU.is_equal), waits=[ev2, eve1], inc="vr")
            evtc = P.op("vector", lambda e: e.tensor_tensor(
                s_ta[:], s_eq[:], s_io64[:], ALU.mult), waits=[eveq2], inc="vr")
            evtd = P.op("vector", lambda e: e.tensor_scalar(
                s_ta[:], s_ta[:], 64.0, None, ALU.add), waits=[evtc], inc="vr")
            evex = P.op("vector", lambda e, tt=tt: e.tensor_scalar(
                s_excl[:], s_m2[:, tt:tt + 1], s_m1[:, tt:tt + 1], None,
                ALU.is_equal), waits=[ev1, ev2], inc="vr")
            evle = P.op("vector", lambda e, tt=tt: e.tensor_scalar(
                s_le[:], s_iota8[:], s_e1[:, tt:tt + 1], None, ALU.is_le),
                waits=[eve1, ev_io], inc="vr")
            evle2 = P.op("vector", lambda e: e.tensor_scalar(
                s_le[:], s_le[:], s_excl[:, 0:1], 1000.0, ALU.mult, ALU.mult),
                waits=[evle, evex], inc="vr")
            evte = P.op("vector", lambda e: e.tensor_tensor(
                s_ta[:], s_ta[:], s_le[:], ALU.add), waits=[evtd, evle2], inc="vr")
            eve2 = P.op("vector", lambda e, tt=tt: e.tensor_reduce(
                s_e2[:, tt:tt + 1], s_ta[:], AX.X, ALU.min),
                waits=[evte], inc="vr")
            evm1n = P.op("vector", lambda e, tt=tt: e.tensor_scalar(
                s_m1n[:, tt:tt + 1], s_m1[:, tt:tt + 1], -1.0, None, ALU.mult),
                waits=[ev1], inc="vr")
            evm2n = P.op("vector", lambda e, tt=tt: e.tensor_scalar(
                s_m2n[:, tt:tt + 1], s_m2[:, tt:tt + 1], -1.0, None, ALU.mult),
                waits=[ev2], inc="vr")
            evg1 = P.op("scalar", lambda e, tt=tt: e.activation(
                out=s_g1[:, tt:tt + 1], in_=s_m1[:, tt:tt + 1], func=AF.Sigmoid,
                bias=s_m2n[:, tt:tt + 1], scale=1.0), waits=[evm2n], inc="sr")
            evg2 = P.op("scalar", lambda e, tt=tt: e.activation(
                out=s_g2[:, tt:tt + 1], in_=s_m2[:, tt:tt + 1], func=AF.Sigmoid,
                bias=s_m1n[:, tt:tt + 1], scale=1.0), waits=[evm1n], inc="sr")
            ev_top[tt] = (eve1, eve2, evg1, evg2)

        all_top = [x for t4 in ev_top for x in t4]

        def rv(tile, pat, **kw):
            return tile[:].rearrange(pat, **kw)

        ev_M = []
        for e8 in range(E):
            for k in range(2):
                src = s_e1 if k == 0 else s_e2
                ev_M.append(P.op("vector", lambda e, e8=e8, k=k, src=src:
                                 e.tensor_scalar(
                    rv(s_M, "p (c e t k) -> p c e t k", c=4, e=8, t=4)[
                        :, :, e8:e8 + 1, :, k:k + 1],
                    rv(src, "p (c o t u) -> p c o t u", c=4, o=1, u=1),
                    float(e8), None, ALU.is_equal),
                    waits=(all_top if (e8 == 0 and k == 0) else []), inc="vr"))
        ev_R2 = P.op("vector", lambda e: e.tensor_tensor(
            s_R2[:],
            rv(s_M, "p (a k) -> p a k", k=2)[:, :, 0:1],
            rv(s_M, "p (a k) -> p a k", k=2)[:, :, 1:2], ALU.add),
            waits=ev_M, inc="vr")
        ev_trm = P.op("tensor", lambda e: e.matmul(
            pb[5][:, 0:128], s_tri[:], s_R2[:], start=True, stop=True),
            waits=[ev_R2, EV["tri"]], inc="mm")
        ev_tripv = P.op("vector", lambda e: e.tensor_copy(s_trip[:], pb[5][:, 0:128]),
                        waits=[ev_trm], inc="vr")
        ev_totm = P.op("tensor", lambda e: e.matmul(
            pb[5][0:1, 0:128], s_onecol[:], s_R2[:], start=True, stop=True),
            waits=[ev_R2, ev_oc, ev_tripv], inc="mm")
        ev_totv = P.op("vector", lambda e: e.tensor_copy(s_tot[:], pb[5][0:1, 0:128]),
                       waits=[ev_totm], inc="vr")
        ev_p1 = P.op("vector", lambda e: e.tensor_copy(s_tinc[:], s_tot[:]),
                     waits=[ev_totv], inc="vr")
        ev_p2 = P.op("vector", lambda e: e.tensor_tensor(
            rv(s_tinc, "p (a t) -> p a t", t=4)[:, :, 1:4],
            rv(s_tot, "p (a t) -> p a t", t=4)[:, :, 1:4],
            rv(s_tot, "p (a t) -> p a t", t=4)[:, :, 0:3], ALU.add),
            waits=[ev_p1], inc="vr")
        ev_p3 = P.op("vector", lambda e: e.tensor_tensor(
            rv(s_tots, "p (a t) -> p a t", t=4)[:, :, 2:4],
            rv(s_tinc, "p (a t) -> p a t", t=4)[:, :, 2:4],
            rv(s_tinc, "p (a t) -> p a t", t=4)[:, :, 0:2], ALU.add),
            waits=[ev_p2], inc="vr")
        ev_p4 = P.op("vector", lambda e: e.tensor_copy(
            rv(s_tots, "p (a t) -> p a t", t=4)[:, :, 0:2],
            rv(s_tinc, "p (a t) -> p a t", t=4)[:, :, 0:2]),
            waits=[ev_p2], inc="vr")
        ev_p5 = P.op("vector", lambda e: e.tensor_tensor(
            s_tots[:], s_tots[:], s_tot[:], ALU.subtract),
            waits=[ev_p3, ev_p4], inc="vr")
        ev_bcm = P.op("tensor", lambda e: e.matmul(
            pb[5][:, 0:128], s_one1[:], s_tots[:], start=True, stop=True),
            waits=[ev_p5, ev_totv], inc="mm")
        ev_pexc = P.op("vector", lambda e: e.tensor_copy(s_pexc[:], pb[5][:, 0:128]),
                       waits=[ev_bcm], inc="vr")
        ev_cn = P.op("vector", lambda e: e.tensor_reduce(
            s_cnt[:], rv(s_tot, "p (a t) -> p a t", t=4), AX.X, ALU.add),
            waits=[ev_totv], inc="vr")
        ev_cd = P.op("gpsimd", lambda e: e.dma_start(
            out=cc_cnt_in.ap()[:, :], in_=s_cnt[:]),
            waits=[ev_cn], inc="gz2", inck=16, dma=True)
        ev_agc = P.op("gpsimd", lambda e: e.collective_compute(
            "AllGather", ALU.bypass, replica_groups=[list(range(8))],
            ins=[cc_cnt_in.ap().opt()], outs=[cc_cnt_out.ap().opt()]),
            waits=[ev_cd], inc="ccc", dma=True)
        ev_ccl = P.op("gpsimd", lambda e: e.dma_start(
            out=s_ccag[:],
            in_=cc_cnt_out.ap().rearrange("r (c e) -> (r c) e", e=8)),
            waits=[ev_agc], inc="ld2", inck=16, dma=True)
        ev_bglm = P.op("tensor", lambda e: e.matmul(
            pb[6][0:4, 0:8], s_m32[:], s_ccag[:], start=True, stop=True),
            waits=[ev_ccl, EV["m32"], ev_bcast_done], inc="mm")
        ev_bglv = P.op("vector", lambda e: e.tensor_copy(s_bgl[:], pb[6][0:4, 0:8]),
                       waits=[ev_bglm], inc="vr")
        ev_bgd = P.op("gpsimd", lambda e: e.dma_start(
            out=bgl_scr.ap().rearrange("a (b c) -> (a b) c", c=8), in_=s_bgl[:]),
            waits=[ev_bglv], inc="gz2", inck=16, dma=True)
        ev_bgl2 = P.op("gpsimd", lambda e: e.dma_start(
            out=s_bglr[:], in_=bgl_scr.ap()[:, :]),
            waits=[ev_bgd], inc="ld2", inck=16, dma=True)
        ev_bglm2 = P.op("tensor", lambda e: e.matmul(
            pb[6][:, 0:32], s_one1[:], s_bglr[:], start=True, stop=True),
            waits=[ev_bgl2, ev_bglv], inc="mm")
        ev_bglbv = P.op("vector", lambda e: e.tensor_copy(s_bglb[:], pb[6][:, 0:32]),
                        waits=[ev_bglm2], inc="vr")
        ev_pk0 = P.op("vector", lambda e: e.tensor_tensor(
            rv(s_pos, "p (a k) -> p a k", k=2)[:, :, 0:1],
            rv(s_trip, "p (a o) -> p a o", o=1),
            rv(s_pexc, "p (a o) -> p a o", o=1), ALU.add),
            waits=[ev_tripv, ev_pexc], inc="vr")
        ev_pk1 = P.op("vector", lambda e: e.tensor_tensor(
            rv(s_pos, "p (a k) -> p a k", k=2)[:, :, 1:2],
            rv(s_pos, "p (a k) -> p a k", k=2)[:, :, 0:1],
            rv(s_M, "p (a k) -> p a k", k=2)[:, :, 0:1], ALU.add),
            waits=[ev_pk0], inc="vr")
        ev_pg = []
        for t4 in range(4):
            for k in range(2):
                ev_pg.append(P.op("vector", lambda e, t4=t4, k=k: e.tensor_tensor(
                    rv(s_pos, "p (c e t k) -> p c e t k", c=4, e=8, t=4)[
                        :, :, :, t4:t4 + 1, k:k + 1],
                    rv(s_pos, "p (c e t k) -> p c e t k", c=4, e=8, t=4)[
                        :, :, :, t4:t4 + 1, k:k + 1],
                    rv(s_bglb, "p (c e o u) -> p c e o u", c=4, o=1, u=1),
                    ALU.add),
                    waits=[ev_pk1, ev_bglbv], inc="vr"))

        def interleave(dst, srck0, srck1, waits):
            e0 = P.op("vector", lambda e: e.tensor_copy(
                rv(dst, "p (c t k) -> p c t k", c=4, k=2)[:, :, :, 0:1],
                rv(srck0, "p (c t u) -> p c t u", c=4, u=1)), waits=waits, inc="vr")
            e1 = P.op("vector", lambda e: e.tensor_copy(
                rv(dst, "p (c t k) -> p c t k", c=4, k=2)[:, :, :, 1:2],
                rv(srck1, "p (c t u) -> p c t u", c=4, u=1)), waits=waits, inc="vr")
            return [e0, e1]

        ev_es = interleave(s_esel, s_e1, s_e2, all_top)
        ev_gs = interleave(s_gsel, s_g1, s_g2, all_top)
        ev_rs = interleave(s_sa, s_rmat, s_rmat, list(ev_r_all))
        ev_ms = interleave(s_sb, s_mrmat, s_mrmat, list(ev_mr_all))
        ev_mp = P.op("vector", lambda e: e.tensor_tensor(
            s_Mp[:], s_M[:], s_pos[:], ALU.mult), waits=ev_pg + ev_M, inc="vr")
        ev_ps = P.op("vector", lambda e: e.tensor_reduce(
            rv(s_psel, "p (c tk) -> p c tk", c=4),
            rv(s_Mp, "p (c e tk) -> p c tk e", c=4, e=8), AX.X, ALU.add),
            waits=[ev_mp], inc="vr")
        ev_vd = P.op("vector", lambda e: e.tensor_scalar(
            s_valid[:], s_psel[:], float(CAP), None, ALU.is_lt),
            waits=[ev_ps], inc="vr")
        pay = rv(s_payl, "p (s f) -> p s f", f=4)
        ev_pay = [
            P.op("vector", lambda e: e.tensor_copy(
                pay[:, :, 0:1], rv(s_tglob, "p (s u) -> p s u", u=1)),
                waits=[EV["tglob"]], inc="vr"),
            P.op("vector", lambda e: e.tensor_copy(
                pay[:, :, 1:2], rv(s_gsel, "p (s u) -> p s u", u=1)),
                waits=ev_gs, inc="vr"),
            P.op("vector", lambda e: e.tensor_copy(
                pay[:, :, 2:3], rv(s_sa, "p (s u) -> p s u", u=1)),
                waits=ev_rs, inc="vr"),
            P.op("vector", lambda e: e.tensor_copy(
                pay[:, :, 3:4], rv(s_sb, "p (s u) -> p s u", u=1)),
                waits=ev_ms, inc="vr"),
        ]
        ev_a1 = P.op("vector", lambda e: e.tensor_scalar(
            s_sa[:], s_esel[:], float(CAPP), None, ALU.mult),
            waits=ev_es + ev_pay, inc="vr")
        ev_a2 = P.op("vector", lambda e: e.tensor_tensor(
            s_sa[:], s_sa[:], s_psel[:], ALU.add), waits=[ev_a1, ev_ps], inc="vr")
        ev_b1 = P.op("vector", lambda e: e.tensor_scalar(
            s_sb[:], s_slotid[:], float(DUMP0), None, ALU.add),
            waits=[ev_sid] + ev_pay, inc="vr")
        ev_d1 = P.op("vector", lambda e: e.tensor_tensor(
            s_sa[:], s_sa[:], s_sb[:], ALU.subtract), waits=[ev_a2, ev_b1], inc="vr")
        ev_d2 = P.op("vector", lambda e: e.tensor_tensor(
            s_sa[:], s_sa[:], s_valid[:], ALU.mult), waits=[ev_d1, ev_vd], inc="vr")
        ev_d3 = P.op("vector", lambda e: e.tensor_tensor(
            s_sa[:], s_sa[:], s_sb[:], ALU.add), waits=[ev_d2], inc="vr")
        ev_dest = P.op("vector", lambda e: e.tensor_copy(s_dest[:], s_sa[:]),
                       waits=[ev_d3], inc="vr")
        ev_y1 = P.op("vector", lambda e: e.tensor_scalar(
            s_sb[:], s_psel[:], -float(ZROW), None, ALU.add),
            waits=[ev_dest, ev_ps], inc="vr")
        ev_y2 = P.op("vector", lambda e: e.tensor_tensor(
            s_sb[:], s_sb[:], s_valid[:], ALU.mult), waits=[ev_y1], inc="vr")
        ev_y3 = P.op("vector", lambda e: e.tensor_scalar(
            s_sb[:], s_sb[:], float(ZROW), None, ALU.add), waits=[ev_y2], inc="vr")
        ev_c1 = P.op("vector", lambda e: e.tensor_scalar(
            s_sa[:], s_sb[:], 1024.0, None, ALU.is_ge),
            waits=[ev_y3, ev_dest], inc="vr")
        ev_c2 = P.op("vector", lambda e: e.tensor_scalar(
            s_valid[:], s_sb[:], 2048.0, None, ALU.is_ge),
            waits=[ev_y3, ev_vd, ev_d2], inc="vr")
        ev_c3 = P.op("vector", lambda e: e.tensor_tensor(
            s_sa[:], s_sa[:], s_valid[:], ALU.add), waits=[ev_c1, ev_c2], inc="vr")
        ev_c4 = P.op("vector", lambda e: e.tensor_scalar(
            s_valid[:], s_sb[:], 3072.0, None, ALU.is_ge),
            waits=[ev_c3], inc="vr")
        ev_c5 = P.op("vector", lambda e: e.tensor_tensor(
            s_sa[:], s_sa[:], s_valid[:], ALU.add), waits=[ev_c4], inc="vr")
        ev_c6 = P.op("vector", lambda e: e.tensor_scalar(
            s_scr32[:], s_sb[:], 3584.0, None, ALU.is_ge),
            waits=[ev_c5], inc="vr")
        ev_c7 = P.op("vector", lambda e: e.tensor_tensor(
            s_sa[:], s_sa[:], s_scr32[:], ALU.add), waits=[ev_c6], inc="vr")
        ev_y4 = P.op("vector", lambda e: e.tensor_scalar(
            s_yidxf[:], s_sa[:], 7168.0, None, ALU.mult), waits=[ev_c7], inc="vr")
        ev_y5 = P.op("vector", lambda e: e.scalar_tensor_tensor(
            s_yidxf[:], s_scr32[:], -3584.0, s_yidxf[:], ALU.mult, ALU.add),
            waits=[ev_y4, ev_c6], inc="vr")
        ev_y5b = P.op("vector", lambda e: e.tensor_tensor(
            s_yidxf[:], s_yidxf[:], s_sb[:], ALU.add), waits=[ev_y5], inc="vr")
        ev_y6 = P.op("vector", lambda e: e.tensor_scalar(
            s_valid[:], s_valid[:], -512.0, 1024.0, ALU.mult, ALU.add),
            waits=[ev_c7], inc="vr")
        ev_y6b = P.op("vector", lambda e: e.scalar_tensor_tensor(
            s_valid[:], s_scr32[:], 264.0, s_valid[:], ALU.mult, ALU.add),
            waits=[ev_y6], inc="vr")
        ev_y7 = P.op("vector", lambda e: e.tensor_tensor(
            s_valid[:], s_valid[:], s_esel[:], ALU.mult),
            waits=ev_es + [ev_y6b], inc="vr")
        ev_y8 = P.op("vector", lambda e: e.tensor_tensor(
            s_yidxf[:], s_yidxf[:], s_valid[:], ALU.add),
            waits=[ev_y5b, ev_y7], inc="vr")
        ev_yidx = P.op("vector", lambda e: e.tensor_copy(s_yidx[:], s_yidxf[:]),
                       waits=[ev_y8], inc="vr")
        ev_scat = []
        for j in range(32):
            ev_scat.append(P.op("gpsimd", lambda e, j=j: e.indirect_dma_start(
                out=tbl_in.ap()[:, :],
                out_offset=IndirectOffsetOnAxis(s_dest[:, j:j + 1], axis=0),
                in_=s_payl[:, j * 4:(j + 1) * 4], in_offset=None),
                waits=[ev_dest] + ev_pay + ev_tz + ev_yz, inc="gs",
                inck=16, dma=True))
        ev_art = P.op("gpsimd", lambda e: e.collective_compute(
            "AllReduce", ALU.add, replica_groups=[list(range(8))],
            ins=[tbl_in.ap().opt()], outs=[tbl_out.ap().opt()]),
            waits=ev_scat, inc="cct", dma=True)

        if stop_after == "routing":
            _finish(P, DBG, debug_taps, SBT, y_out, [ev_art, ev_agkv])
            _emit(P, nc, es)
            return nc

        kvflat = kv_all.ap().rearrange("(a b) -> a b", b=1)
        ev_kld = []
        for kb in range(8):
            evlo = P.op("gpsimd", lambda e, kb=kb: e.indirect_dma_start(
                out=s_kall[0:64, kb * 512:(kb + 1) * 512],
                out_offset=None, in_=kvflat,
                in_offset=IndirectOffsetOnAxis(s_kidx[0:64, kb:kb + 1], axis=0)),
                waits=[ev_agkv, EV["kidx"]], inc="gk", inck=16, dma=True)
            ev_kld.append(P.op("gpsimd", lambda e, kb=kb: e.dma_start(
                out=s_kall[64:128, kb * 512:(kb + 1) * 512],
                in_=s_kall[0:64, kb * 512:(kb + 1) * 512]),
                waits=[evlo], inc="gk", inck=16, dma=True))
        ev_vld = []
        for i in range(32):
            ev_vld.append(P.op("gpsimd", lambda e, i=i: e.indirect_dma_start(
                out=s_vall[:, i * 65:(i + 1) * 65], out_offset=None,
                in_=kvflat,
                in_offset=IndirectOffsetOnAxis(s_vidx[:, i:i + 1], axis=0)),
                waits=[ev_agkv, EV["vidx"]], inc="gv", inck=16, dma=True))

        ev_kld_total = ev_kld[-1]
        ev_vld_total = ev_vld[-1]
        pair = 0
        ev_stt = [None] * NPAIR
        ev_exp = [None] * NPAIR
        ev_av = [None] * NPAIR
        ev_zev_bb = [None] * 4
        ev_rec_tt = [None] * 16
        prev_rec = None
        prev_zev = None
        SCB = [0, 2, 3, 4]
        ev_qb_all = ev_qb[-1]

        def emit_av(p, i0, n_k):
            i = i0
            ev_av[p] = P.op("tensor", lambda e, i=i, p=p,
                            st=(i == 0), sp=(i == n_k - 1): e.matmul(
                pb[1][0:65, 0:512], s_vall[:, i * 65:(i + 1) * 65],
                s_pt[p % 4][:], start=st, stop=sp),
                waits=[ev_exp[p], ev_vld_total,
                       prev_zev if i == 0 else None], inc="mm")

        for jj in range(4):
            bb = 3 - jj
            qoff = bb * 512
            n_k = ATT_BOUNDS[jj]
            base = pair
            for ii in range(n_k // 2):
                for half in range(2):
                    p = pair
                    i = 2 * ii + half
                    bank = pb[SCB[p % 4]]
                    kt = s_kall[half * 64:(half + 1) * 64,
                                i * 128:(i + 1) * 128]
                    qt = s_q[half * 64:(half + 1) * 64, qoff:qoff + 512]
                    evsc = P.op("tensor", lambda e, kt=kt, qt=qt, bank=bank,
                                tp=(half * 64, 0): e.matmul(
                        bank[:, 0:512], kt, qt, start=True, stop=True,
                        tile_position=tp),
                        waits=[ev_kld_total, ev_qev_all[bb], ev_qb_all,
                               ev_stt[p - 4] if p >= 4 else None,
                               ev_qk_free if p <= 3 else None,
                               ev_vg_free if p <= 3 else None,
                               ev_logit[15] if p <= 3 else None,
                               prev_fm.get(p % 2) if p <= 3 else None],
                        inc="mm")
                    evmk = P.op("vector", lambda e, p=p: e.tensor_scalar(
                        s_mask[:], s_dmat[:], s_dtbl[:, p:p + 1], None,
                        ALU.is_gt),
                        waits=[EV["dmat"], EV["dtbl"],
                               ev_stt[p - 1] if p >= 1 else None], inc="vr")
                    ev_stt[p] = P.op("vector", lambda e, p=p, bank=bank:
                                     e.scalar_tensor_tensor(
                        s_u[p % 4][:], s_mask[:], MASKV, bank[:, 0:512],
                        ALU.mult, ALU.add),
                        waits=[evmk, evsc,
                               ev_exp[p - 4] if p >= 4 else None], inc="vr")
                    ev_exp[p] = P.op("scalar", lambda e, p=p: e.activation(
                        out=s_pt[p % 4][:], in_=s_u[p % 4][:], func=AF.Exp,
                        bias=0.0, scale=0.125),
                        waits=[ev_stt[p], ev_c125,
                               ev_av[p - 4] if p >= 4 else None], inc="sr")
                    pair += 1
                if ii >= 1:
                    emit_av(base + 2 * (ii - 1), 2 * (ii - 1), n_k)
                    emit_av(base + 2 * (ii - 1) + 1, 2 * (ii - 1) + 1, n_k)
            emit_av(base + n_k - 2, n_k - 2, n_k)
            emit_av(base + n_k - 1, n_k - 1, n_k)
            ev_zev_bb[bb] = P.op("vector", lambda e, qoff=qoff: e.tensor_copy(
                s_z[:, qoff:qoff + 512], pb[1][0:65, 0:512]),
                waits=[ev_av[pair - 1]], inc="vr")
            prev_zev = ev_zev_bb[bb]
            for ti in range(4):
                tt = bb * 4 + ti
                evdt = P.op("tensor", lambda e, qoff=qoff, ti=ti: e.matmul(
                    pb[7][:, 0:1],
                    s_z[64:65, qoff + ti * 128:qoff + (ti + 1) * 128],
                    s_onecol[64:65, 0:1], start=True, stop=True),
                    waits=[ev_zev_bb[bb], ev_oc, prev_rec], inc="mm")
                prev_rec = P.op("vector", lambda e, tt=tt: e.reciprocal(
                    s_recip[:, tt:tt + 1], pb[7][:, 0:1]),
                    waits=[evdt], inc="vr")
                ev_rec_tt[tt] = prev_rec

        if stop_after == "attention":
            _finish(P, DBG, debug_taps, SBT, y_out,
                    [ev_art] + ev_rec_tt + list(ev_zev_bb))
            _emit(P, nc, es)
            return nc

        ev_tg = []
        for j in range(NST):
            ev_tg.append(P.op("gpsimd", lambda e, j=j: e.indirect_dma_start(
                out=s_trow[:, j * 4:(j + 1) * 4], out_offset=None,
                in_=tbl_out.ap()[:, :],
                in_offset=IndirectOffsetOnAxis(s_tslice[:, j:j + 1], axis=0)),
                waits=[ev_art, EV["tslice"]], inc="gt", inck=16, dma=True))
        ev_tc = P.op("vector", lambda e: e.tensor_copy(
            s_tcast[:], rv(s_trow, "p (j f) -> p j f", f=4)[:, :, 0:1]),
            waits=[ev_tg[-1]], inc="vr")

        prev_fm[("x", 0)] = ev_stt[NPAIR - 1]
        prev_fm[("x", 1)] = ev_zev_bb[0]
        ev_lnx = [None] * NST
        ev_xgf_done = [None] * NST
        ev_gelu_hist = []
        ev_hsplit_hist = []
        ev_gel_rd = [None]
        ev_w1_user = {}
        ev_w2_hist = []
        ev_ydma = []
        ev_ydma_sg = {}
        ev_ccy = []
        ev_yev_used = [None, None]
        prev_sg_y = [None, None, None, ev_rec_tt[0]]
        prev_sg_y[3] = ev_rec_tt[3]
        sgbase = 0
        for sgi, w in enumerate(SGW):
            ntl = w // 128
            j0 = sgbase // 128
            for ti in range(ntl):
                j = j0 + ti
                evg = P.op("gpsimd", lambda e, j=j: e.indirect_dma_start(
                    out=s_xg[j % 4][:], out_offset=None,
                    in_=DI["x_full"][:, :],
                    in_offset=IndirectOffsetOnAxis(s_tcast[:, j:j + 1], axis=0)),
                    waits=[ev_tc, ev_lnx[j - 4] if j >= 4 else None],
                    inc=f"gx{j % 4}", inck=16, dma=True)
                ev_lnx[j] = P.op("vector", lambda e, j=j: e.tensor_scalar(
                    s_xng[:, (j % 4) * D:(j % 4 + 1) * D], s_xg[j % 4][:],
                    s_trow[:, j * 4 + 2:j * 4 + 3],
                    s_trow[:, j * 4 + 3:j * 4 + 4], ALU.mult, ALU.add),
                    waits=[evg, ev_tg[-1],
                           ev_xgf_done[j - 4] if j >= 4 else ev_tile_tp[12 + (j % 4)]],
                    inc="vr")
            evfm = []
            for ti in range(ntl):
                j = j0 + ti
                for dcc in range(8):
                    bank = (ti * 8 + dcc) % 2
                    evt = P.op("tensor", lambda e, j=j, dcc=dcc, bank=bank:
                               e.transpose(
                        pb[bank][:, 0:128],
                        s_xng[:, (j % 4) * D + dcc * 128:(j % 4) * D + (dcc + 1) * 128],
                        s_id[:]),
                        waits=[ev_lnx[j], prev_fm.get(("x", bank))], inc="tp")
                    evhh = P.op(
                        "vector", lambda e, ti=ti, dcc=dcc, bank=bank: e.tensor_copy(
                            s_xgfh[:, dcc * 512 + ti * 128: dcc * 512 + (ti + 1) * 128],
                            pb[bank][:, 0:128]), waits=[evt], inc="fm")
                    prev_fm[("x", bank)] = P.op(
                        "vector", lambda e, ti=ti, dcc=dcc, bank=bank: e.tensor_tensor(
                            s_xgfl[:, dcc * 512 + ti * 128: dcc * 512 + (ti + 1) * 128],
                            pb[bank][:, 0:128],
                            s_xgfh[:, dcc * 512 + ti * 128: dcc * 512 + (ti + 1) * 128],
                            ALU.subtract), waits=[evhh], inc="fm")
                    evfm.append(prev_fm[("x", bank)])
                ev_xgf_done[j] = evfm[-1]
            for ti in range(ntl):
                ev_xgf_done[j0 + ti] = evfm[-1]
            ev_w1g = {}

            w1v = w1s.ap().rearrange("p (d g h c) -> p d g h c",
                                     d=8, g=16, h=2, c=256)

            def load_w1(gq):
                gqg = sgi * 16 + gq
                spl = [SPLIT_EV[(0, dcc * 128, (gq * 256) // 512 * 512)]
                       for dcc in range(8)]
                ev_w1g[gq] = P.op("sync", lambda e, gq=gq: e.dma_start(
                    out=s_w1hl[:, (gq % 2) * 4096:(gq % 2 + 1) * 4096],
                    in_=w1v[:, :, gq:gq + 1, :, :]),
                    waits=[ev_w1_user.get(gqg - 2)] + spl,
                    inc="w1a" if gq % 2 == 0 else "w1b", inck=16, dma=True)

            load_w1(0)
            load_w1(1)
            ev_hmm_last = {}
            for hc in range(32):
                gq = hc // 2
                hb = 2 + (hc % 2)
                hgi = sgi * 32 + hc
                lastm = None
                np3 = 0
                for hilo, xs in ((0, s_xgfh), (0, s_xgfl), (1, s_xgfh)):
                    for dcc in range(8):
                        woff = ((gq % 2) * 4096 + dcc * 512 + hilo * 256
                                + (hc % 2) * 128)
                        lastm = P.op("tensor", lambda e, hc=hc, dcc=dcc, hb=hb,
                                     w=w, woff=woff, xs=xs, np3=np3: e.matmul(
                            pb[hb][:, 0:w],
                            s_w1hl[:, woff:woff + 128],
                            xs[:, dcc * 512:dcc * 512 + w],
                            start=(np3 == 0 and dcc == 0),
                            stop=(np3 == 2 and dcc == 7)),
                            waits=([ev_w1g[gq], evfm[-1],
                                    ev_gelu_hist[hgi - 2] if hgi >= 2 else None,
                                    ev_qk_free if hgi == 0 else None,
                                    ev_stt[NPAIR - 1] if hgi <= 1 else None]
                                   if (np3 == 0 and dcc == 0) else []),
                            inc="mm" if (np3 == 2 and dcc == 7) else None)
                    np3 += 1
                ev_hmm_last[hc] = lastm
                arena_guard = ([ev_av[NPAIR - 1], ev_stt[NPAIR - 1],
                                ev_kdma[-1], ev_vdma[-1]]
                               if (sgi == 0 and hc == 0) else [])
                evgel = P.op("scalar", lambda e, hc=hc, hb=hb, w=w:
                             e.activation(
                    out=s_gel[:, 0:w], in_=pb[hb][:, 0:w],
                    func=AF.Gelu_apprx_tanh, bias=s_b1c[:, hc:hc + 1], scale=1.0),
                    waits=[lastm, EV["b1c"], ev_gel_rd[0]] + arena_guard,
                    inc="sr")
                ev_gelu_hist.append(evgel)
                evhh2 = P.op("vector", lambda e, hc=hc, w=w: e.tensor_copy(
                    s_hh[:, hc * 512:hc * 512 + w], s_gel[:, 0:w]),
                    waits=[evgel], inc="fm")
                ev_gel_rd[0] = P.op("vector", lambda e, hc=hc, w=w:
                                    e.tensor_tensor(
                    s_hl[:, hc * 512:hc * 512 + w], s_gel[:, 0:w],
                    s_hh[:, hc * 512:hc * 512 + w], ALU.subtract),
                    waits=[evhh2], inc="fm")
                ev_hsplit_hist.append(ev_gel_rd[0])
                if sgi == 0:
                    emit_split(4)
                if hc % 2 == 1:
                    ev_w1_user[sgi * 16 + hc // 2] = lastm
                    if hc // 2 + 2 < 16:
                        load_w1(hc // 2 + 2)
            w2v = w2s.ap().rearrange("p (t g hl) -> p t g hl", t=32, g=2)
            for dg in range(2):
                for hc in range(32):
                    w2i = len(ev_w2_hist)
                    spl_ev = SPLIT_EV[(1, hc * 128, dg * 512)]
                    evw2 = P.op("sync", lambda e, hc=hc, dg=dg: e.dma_start(
                        out=s_w2hl[:, (hc % 2) * 1024:(hc % 2 + 1) * 1024],
                        in_=w2v[:, hc:hc + 1, dg:dg + 1, :]),
                        waits=([ev_w2_hist[w2i - 2][1]] if w2i >= 2 else []
                               ) + [spl_ev],
                        inc="w2a" if hc % 2 == 0 else "w2b", inck=16, dma=True)
                    lastm = None
                    np3 = 0
                    for hs, hilo in ((s_hh, 0), (s_hh, 1), (s_hl, 0)):
                        for st in range(ntl):
                            lastm = P.op("tensor", lambda e, hc=hc, st=st,
                                         hs=hs, hilo=hilo, np3=np3: e.matmul(
                                pb[4 + st][:, 0:512],
                                hs[:, hc * 512 + st * 128:hc * 512 + (st + 1) * 128],
                                s_w2hl[:, (hc % 2) * 1024 + hilo * 512:
                                       (hc % 2) * 1024 + hilo * 512 + 512],
                                start=(hc == 0 and np3 == 0),
                                stop=(hc == 31 and np3 == 2)),
                                waits=([ev_hsplit_hist[sgi * 32 + hc], evw2] +
                                       ([prev_sg_y[st]]
                                        if (hc == 0 and np3 == 0) else [])),
                                inc="mm" if (st == ntl - 1 and np3 == 2) else None)
                        np3 += 1
                    ev_w2_hist.append((evw2, lastm))
                for st in range(ntl):
                    eve = P.op("vector", lambda e, st=st, dg=dg: e.tensor_tensor(
                        s_yev[st % 2][:], pb[4 + st][:, 0:512],
                        s_b2bc[:, dg * 512:(dg + 1) * 512], ALU.add),
                        waits=[lastm, ev_yev_used[st % 2], ev_bcast_done],
                        inc="vr")
                    prev_sg_y[st] = eve
                    evd = P.op("sync", lambda e, st=st, dg=dg,
                               r0=sgbase + st * 128: e.dma_start(
                        out=y_in.ap()[r0:r0 + 128, dg * 512:(dg + 1) * 512],
                        in_=s_yev[st % 2][:]),
                        waits=[eve], inc="yya" if st % 2 == 0 else "yyb",
                        inck=16, dma=True)
                    ev_yev_used[st % 2] = evd
                    ev_ydma.append(evd)
                    ev_ydma_sg.setdefault(sgi, []).append(evd)
            if sgi in (1, 3, 5):
                c = (sgi - 1) // 2
                ev_ccy.append(P.op("gpsimd", lambda e, c=c: e.collective_compute(
                    "AllGather", ALU.bypass, replica_groups=[list(range(8))],
                    ins=[y_in.ap()[1024 * c:1024 * (c + 1), :].opt()],
                    outs=[y_all.ap()[8192 * c:8192 * (c + 1), :].opt()]),
                    waits=ev_ydma_sg[sgi - 1] + ev_ydma_sg[sgi],
                    inc=f"ccy{c}", dma=True))
            if sgi == 6:
                ev_ccy.append(P.op("gpsimd", lambda e: e.collective_compute(
                    "AllGather", ALU.bypass, replica_groups=[list(range(8))],
                    ins=[y_in.ap()[3072:3584, :].opt()],
                    outs=[y_all.ap()[24576:28672, :].opt()]),
                    waits=ev_ydma_sg[6], inc="ccy3", dma=True))
            if sgi == 8:
                ev_ccy.append(P.op("gpsimd", lambda e: e.collective_compute(
                    "AllGather", ALU.bypass, replica_groups=[list(range(8))],
                    ins=[y_in.ap()[3584:YROWS, :].opt()],
                    outs=[y_all.ap()[28672:8 * YROWS, :].opt()]),
                    waits=ev_ydma_sg[7] + ev_ydma_sg[8] + ev_yz,
                    inc="ccy4", dma=True))
            sgbase += w

        if stop_after == "expert":
            _finish(P, DBG, debug_taps, SBT, y_out, list(ev_ccy))
            _emit(P, nc, es)
            return nc

        ev_pst = [None] * 16
        ev_xtu = None
        ev_xbu = None
        ev_xl_prev = [ev_xload_prev[0], ev_xload_prev[1]]
        ev_zo_read = None
        ev_zo_free = [prev_fm[("x", 0)], prev_fm[("x", 1)]]
        for tt in range(16):
            evx = P.op("scalar", lambda e, tt=tt: e.dma_start(
                out=s_xt[tt % 2][:], in_=DI["x_own"][tt * 128:(tt + 1) * 128, :]),
                waits=[ev_xtu, ev_xn[14 + (tt % 2)] if tt < 2 else None,
                       ev_xl_prev[tt % 2]],
                inc="dmaA" if tt % 2 == 0 else "dmaB", inck=16, dma=True)
            ev_xl_prev[tt % 2] = evx
            evxb = P.op("gpsimd", lambda e, tt=tt: e.tensor_tensor(
                s_xb[:], s_xt[tt % 2][:], s_bobc[:], ALU.add),
                waits=[evx, ev_bcast_done, ev_xbu], inc="gc")
            ev_zoev = []
            for dg in range(2):
                evzm = P.op("tensor", lambda e, tt=tt, dg=dg: e.matmul(
                    pb[dg][:, 0:512], s_z[0:64, tt * 128:(tt + 1) * 128],
                    s_wo[:, dg * 512:(dg + 1) * 512], start=True, stop=True),
                    waits=[ev_zev_bb[tt // 4], EV["wo"], ev_zo_free[dg]],
                    inc="mm")
                ev_zoev.append(P.op("scalar", lambda e, tt=tt, dg=dg: e.activation(
                    out=s_zo[:, dg * 512:(dg + 1) * 512], in_=pb[dg][:, 0:512],
                    func=AF.Identity, bias=0.0, scale=s_recip[:, tt:tt + 1]),
                    waits=[evzm, ev_rec_tt[tt], ev_zo_read], inc="sr"))
                ev_zo_free[dg] = ev_zoev[-1]
            evp = P.op("vector", lambda e: e.tensor_tensor(
                s_out[:], s_zo[:], s_xb[:], ALU.add),
                waits=ev_zoev + [evxb,
                                 ev_pst[tt - 1] if tt >= 1 else None],
                inc="vr")
            ev_zo_read = evp
            ev_xbu = evp
            ev_xtu = evp
            ev_pst[tt] = P.op("scalar", lambda e, tt=tt: e.dma_start(
                out=pscr.ap()[tt * 128:(tt + 1) * 128, :], in_=s_out[:]),
                waits=[evp], inc="yya" if tt % 2 == 0 else "yyb",
                inck=16, dma=True)

        ev_outd = []
        ev_g0u = [None, None]
        ev_g1u = [None, None]
        ev_g0ev = [None, None]
        ev_g1ev = [None, None]
        ev_plev = [None, None]
        ev_pbu = None
        ev_accu = None
        for tt in range(16):
            c4, t4 = divmod(tt, 4)
            f0 = c4 * 8 + t4 * 2
            evy0 = P.op("gpsimd", lambda e, f0=f0, tt=tt: e.indirect_dma_start(
                out=s_yg0[tt % 2][:], out_offset=None, in_=y_all.ap()[:, :],
                in_offset=IndirectOffsetOnAxis(s_yidx[:, f0:f0 + 1], axis=0)),
                waits=ev_ccy + [ev_yidx, ev_g0u[tt % 2], ev_g0ev[tt % 2]],
                inc="gy0a" if tt % 2 == 0 else "gy0b", inck=16, dma=True)
            ev_g0ev[tt % 2] = evy0
            evy1 = P.op("gpsimd", lambda e, f0=f0, tt=tt: e.indirect_dma_start(
                out=s_yg1[tt % 2][:], out_offset=None, in_=y_all.ap()[:, :],
                in_offset=IndirectOffsetOnAxis(s_yidx[:, f0 + 1:f0 + 2], axis=0)),
                waits=ev_ccy + [ev_yidx, ev_g1u[tt % 2], ev_g1ev[tt % 2]],
                inc="gy1a" if tt % 2 == 0 else "gy1b", inck=16, dma=True)
            ev_g1ev[tt % 2] = evy1
            evpl = P.op("sync", lambda e, tt=tt: e.dma_start(
                out=s_xb[:], in_=pscr.ap()[tt * 128:(tt + 1) * 128, :]),
                waits=ev_ccy + [ev_pst[15], ev_pbu, ev_plev[tt % 2]],
                inc="dmaA" if tt % 2 == 0 else "dmaB", inck=16, dma=True)
            ev_plev[tt % 2] = evpl
            eva = P.op("vector", lambda e, tt=tt: e.tensor_scalar(
                s_acc[:], s_yg0[tt % 2][:], s_g1[:, tt:tt + 1], None, ALU.mult),
                waits=[evy0, ev_top[tt][2], ev_accu], inc="vr")
            evb = P.op("vector", lambda e, tt=tt: e.scalar_tensor_tensor(
                s_acc[:], s_yg1[tt % 2][:], s_g2[:, tt:tt + 1], s_acc[:],
                ALU.mult, ALU.add),
                waits=[eva, evy1, ev_top[tt][3]], inc="vr")
            ev_g0u[tt % 2] = evb
            ev_g1u[tt % 2] = evb
            evd2 = P.op("vector", lambda e: e.tensor_tensor(
                s_out[:], s_acc[:], s_xb[:], ALU.add),
                waits=[evb, evpl,
                       ev_outd[-1] if ev_outd else ev_pst[15]], inc="vr")
            ev_pbu = evd2
            ev_accu = evd2
            ev_outd.append(P.op("sync", lambda e, tt=tt: e.dma_start(
                out=y_out[tt * 128:(tt + 1) * 128, :], in_=s_out[:]),
                waits=[evd2], inc="oo", inck=16, dma=True))

        _finish(P, DBG, debug_taps, SBT, y_out, ev_outd)
        _emit(P, nc, es)
    return nc


def _finish(P, DBG, debug_taps, SBT, y_out, final_evs):
    evs = [e for e in final_evs if e is not None]
    for (dname, sname, shape) in debug_taps:
        tile = SBT[sname]
        evs.append(P.op("gpsimd", lambda e, d=DBG[dname], t=tile, s=shape:
                        e.dma_start(out=d[:, :], in_=t[0:s[0], 0:s[1]]),
                        waits=list(evs), inc="gz2", inck=16, dma=True))
    if not any(e.sem == "oo" for e in evs):
        z = SBT["s_zero"]
        for ttt in range(16):
            for hf in range(2):
                evs.append(P.op("gpsimd", lambda e, ttt=ttt, hf=hf: e.dma_start(
                    out=y_out[ttt * 128:(ttt + 1) * 128, hf * 512:(hf + 1) * 512],
                    in_=z[:, 0:512]), inc="gz2", inck=16, dma=True))
    P.op("gpsimd", lambda e: e.nop(), waits=evs)


def _emit(P, nc, es):
    blk = es.enter_context(nc.Block())
    semh = {}
    for nm in P.sem_names:
        semh[nm] = es.enter_context(nc.semaphore(nm))

    def emit_for(engname):
        def body(eng):
            for item in P.items[engname]:
                if item[0] == "wait":
                    eng.wait_ge(semh[item[1]], item[2])
                elif item[0] == "drain":
                    eng.drain()
                else:
                    instr = item[1](eng)
                    for (sm, k) in item[2]:
                        instr.then_inc(semh[sm], k)
        return body

    for engname in ENGS:
        getattr(blk, engname)(emit_for(engname))


def kernel(**inputs):
    in_maps = _shard_inputs(inputs)
    nc = build_program()
    res = run_bass_kernel_spmd(nc, in_maps, core_ids=list(range(NCORE)))
    return _unshard_output(res.results)


# version 13 (final)
# speedup vs baseline: 1.0202x; 1.0202x over previous
import numpy as np
from contextlib import ExitStack

import concourse.bass as bass
import concourse.mybir as mybir
from concourse.bass_utils import run_bass_kernel_spmd
from concourse.bass import IndirectOffsetOnAxis

F32 = mybir.dt.float32
I32 = mybir.dt.int32
AF = mybir.ActivationFunctionType
ALU = mybir.AluOpType
AX = mybir.AxisListType

B, S, D, H = 4, 4096, 1024, 64
NT = B * S
NCORE = 8
TPC = NT // NCORE
E = 8
HD = 4096
CAP = 4301
CAPP = 4352
NST = CAPP // 128
YROWS = 4360
ZROW = 4352
DUMP0 = E * CAPP
NTBL = E * CAPP + 4096
JSETS = [[0, 3, 5, 6], [1, 2, 4, 7]]
ATT_BOUNDS = [32, 24, 16, 8]
NPAIR = sum(ATT_BOUNDS)
MASKV = -8.0e11
SGW = [512] * 8 + [256]

ENGS = ("sync", "tensor", "vector", "scalar", "gpsimd")


class Ev:
    __slots__ = ("sem", "val", "eng", "seq", "dma")

    def __init__(self, sem, val, eng, seq, dma):
        self.sem, self.val, self.eng, self.seq, self.dma = sem, val, eng, seq, dma


class Prog:
    def __init__(self):
        self.items = {e: [] for e in ENGS}
        self.semcnt = {}
        self.seq = {e: 0 for e in ENGS}
        self.last_drain = {e: -1 for e in ENGS}
        self.sem_names = []

    def sem(self, name):
        self.semcnt[name] = 0
        self.sem_names.append(name)

    def op(self, eng, fn, waits=(), inc=None, inck=1, dma=False):
        il = self.items[eng]
        bysem = {}
        need_drain = False
        for w in waits:
            if w is None:
                continue
            if w.eng == eng and not w.dma:
                if w.seq > self.last_drain[eng]:
                    need_drain = True
            else:
                if bysem.get(w.sem, -1) < w.val:
                    bysem[w.sem] = w.val
        if need_drain:
            self.last_drain[eng] = self.seq[eng] - 1
            il.append(("drain",))
        for sm, v in sorted(bysem.items()):
            il.append(("wait", sm, v))
        incs = []
        ev = None
        if inc is not None:
            self.semcnt[inc] += inck
            incs = [(inc, inck)]
            ev = Ev(inc, self.semcnt[inc], eng, self.seq[eng], dma)
        il.append(("op", fn, incs))
        self.seq[eng] += 1
        return ev


def _owned_blocks(c):
    b, p = divmod(c, 2)
    return [(b, J) for J in JSETS[p]]


def _host_constants(core):
    b, p = divmod(core, 2)
    jset = JSETS[p]
    cst = {}
    cst["ident"] = np.eye(128, dtype=np.float32)
    cst["tri"] = np.triu(np.ones((128, 128), np.float32), 1)
    cst["dmat"] = (np.arange(128, dtype=np.float32)[:, None]
                   - np.arange(512, dtype=np.float32)[None, :])
    dt = np.zeros((128, NPAIR), np.float32)
    pair = 0
    for jj in range(4):
        J = jset[3 - jj]
        for i in range(ATT_BOUNDS[jj]):
            dt[:, pair] = 512.0 * J - 128.0 * i
            pair += 1
    cst["dtbl"] = dt
    gid_of_row = []
    for r in range(NCORE):
        for (bb, J) in _owned_blocks(r):
            gid_of_row.append(bb * 8 + J)
    my_gids = [b * 8 + J for J in jset]
    m32 = np.zeros((32, 4), np.float32)
    for row, g in enumerate(gid_of_row):
        for jj, gme in enumerate(my_gids):
            if g < gme:
                m32[row, jj] = 1.0
    cst["m32"] = m32
    KVLEN = 64 * 2048 + 2048 * 65
    kidx = np.zeros((64, 8), np.int32)
    for kb in range(8):
        pr = 0 if kb in JSETS[0] else 1
        r = 2 * b + pr
        bb = JSETS[pr].index(kb)
        for pp in range(64):
            kidx[pp, kb] = r * KVLEN + pp * 2048 + bb * 512
    cst["kidx"] = kidx
    vidx = np.zeros((128, 32), np.int32)
    for vb in range(32):
        for pp in range(128):
            sv = 128 * vb + pp
            J, u = divmod(sv, 512)
            pr = 0 if J in JSETS[0] else 1
            lt = 512 * JSETS[pr].index(J) + u
            vidx[pp, vb] = (2 * b + pr) * KVLEN + 64 * 2048 + lt * 65
    cst["vidx"] = vidx
    tsl = np.zeros((128, NST), np.int32)
    for j in range(NST):
        tsl[:, j] = core * CAPP + 128 * j + np.arange(128)
    cst["tslice"] = tsl
    tg = np.zeros((128, 32), np.float32)
    for c in range(4):
        g = my_gids[c]
        for tt4 in range(4):
            for k in range(2):
                tg[:, c * 8 + tt4 * 2 + k] = g * 512 + tt4 * 128 + np.arange(128)
    cst["tglob"] = tg
    return cst


def _shard_inputs(inputs):
    x = np.asarray(inputs["L0_attn_x"], dtype=np.float32)
    xf = np.ascontiguousarray(x.reshape(NT, D))
    wq = np.asarray(inputs["L0_attn_wq"], np.float32)
    wk = np.asarray(inputs["L0_attn_wk"], np.float32)
    wv = np.asarray(inputs["L0_attn_wv"], np.float32)
    wg = np.asarray(inputs["L0_ffn_Wg"], np.float32)
    wqk = np.ascontiguousarray(np.concatenate([wq, wk], axis=1))
    wvg = np.ascontiguousarray(np.concatenate([wv, wg], axis=1))
    bqk = np.ascontiguousarray(np.concatenate(
        [np.asarray(inputs["L0_attn_bq"], np.float32),
         np.asarray(inputs["L0_attn_bk"], np.float32)]).reshape(128, 1))
    bvg = np.ascontiguousarray(np.concatenate(
        [np.asarray(inputs["L0_attn_bv"], np.float32),
         np.zeros(8, np.float32)]).reshape(72, 1))
    wo = np.ascontiguousarray(np.asarray(inputs["L0_attn_wo"], np.float32))
    bo = np.ascontiguousarray(
        np.asarray(inputs["L0_attn_bo"], np.float32).reshape(1, D))
    W1 = np.asarray(inputs["L0_ffn_W1"], np.float32)
    W2 = np.asarray(inputs["L0_ffn_W2"], np.float32)
    b1 = np.asarray(inputs["L0_ffn_b1"], np.float32)
    b2 = np.asarray(inputs["L0_ffn_b2"], np.float32)
    in_maps = []
    for c in range(NCORE):
        cst = _host_constants(c)
        rows = [xf[bb * S + J * 512: bb * S + (J + 1) * 512]
                for (bb, J) in _owned_blocks(c)]
        im = dict(
            x_own=np.ascontiguousarray(np.concatenate(rows, axis=0)),
            x_full=xf,
            wqk=wqk, wvg=wvg, bqk=bqk, bvg=bvg, wo=wo, bo_row=bo,
            w1=np.ascontiguousarray(W1[c]), w2=np.ascontiguousarray(W2[c]),
            b1c=np.ascontiguousarray(b1[c].reshape(32, 128).T),
            b2row=np.ascontiguousarray(b2[c].reshape(1, D)),
            **cst)
        in_maps.append(im)
    return in_maps


def _unshard_output(results):
    out = np.empty((NT, D), np.float32)
    for c in range(NCORE):
        y = np.asarray(results[c]["y_out"])
        for j, (bb, J) in enumerate(_owned_blocks(c)):
            out[bb * S + J * 512: bb * S + (J + 1) * 512] = y[j * 512:(j + 1) * 512]
    return out.reshape(B, S, D)


def build_program(stop_after=None, debug_taps=()):
    nc = bass.Bass("TRN2", target_bir_lowering=False, debug=False)
    P = Prog()

    DI = {}
    for name, shape, dt in [
        ("x_own", [TPC, D], F32), ("x_full", [NT, D], F32),
        ("wqk", [D, 128], F32), ("wvg", [D, 72], F32),
        ("bqk", [128, 1], F32), ("bvg", [72, 1], F32),
        ("wo", [64, D], F32), ("bo_row", [1, D], F32),
        ("w1", [D, HD], F32), ("w2", [HD, D], F32),
        ("b1c", [128, 32], F32), ("b2row", [1, D], F32),
        ("ident", [128, 128], F32), ("tri", [128, 128], F32),
        ("dmat", [128, 512], F32), ("dtbl", [128, NPAIR], F32),
        ("m32", [32, 4], F32), ("kidx", [64, 8], I32),
        ("vidx", [128, 32], I32), ("tslice", [128, NST], I32),
        ("tglob", [128, 32], F32),
    ]:
        DI[name] = nc.dram_tensor(name, shape, dt, kind="ExternalInput").ap()
    y_out = nc.dram_tensor("y_out", [TPC, D], F32, kind="ExternalOutput").ap()
    DBG = {}
    for (dname, sname, shape) in debug_taps:
        DBG[dname] = nc.dram_tensor(dname, shape, F32, kind="ExternalOutput").ap()

    cc_cnt_in = nc.dram_tensor("cc_cnt_in", [1, 32], F32)
    cc_cnt_out = nc.dram_tensor("cc_cnt_out", [8, 32], F32, addr_space="Shared")
    bgl_scr = nc.dram_tensor("bgl_scr", [1, 32], F32)
    KVLEN = 64 * 2048 + 2048 * 65
    kv_in = nc.dram_tensor("kv_in", [KVLEN], F32)
    kv_all = nc.dram_tensor("kv_all", [8 * KVLEN], F32, addr_space="Shared")
    pscr = nc.dram_tensor("pscr", [TPC, D], F32)
    w1s = nc.dram_tensor("w1s", [128, 65536], mybir.dt.bfloat16)
    w2s = nc.dram_tensor("w2s", [128, 65536], mybir.dt.bfloat16)
    tbl_in = nc.dram_tensor("tbl_in", [NTBL, 4], F32)
    tbl_out = nc.dram_tensor("tbl_out", [NTBL, 4], F32, addr_space="Shared")
    y_in = nc.dram_tensor("y_in", [YROWS, D], F32)
    y_all = nc.dram_tensor("y_all", [8 * YROWS, D], F32, addr_space="Shared")
    DRAMT = {"y_in": y_in, "y_all": y_all, "tbl_out": tbl_out}

    es = ExitStack()
    with es:
        SBT = {}
        _cursor = [20480]

        SBOFF = {}

        def SB(n, s, dt=F32, at=None):
            nbytes = int(np.prod(s[1:])) * mybir.dt.size(dt) if len(s) > 1 else 4
            if at is None:
                off = _cursor[0]
                _cursor[0] = (off + nbytes + 63) // 64 * 64
            else:
                off = at
            SBOFF[n] = off
            SBT[n] = nc.alloc_sbuf_tensor_at(n, s, dt, offset=off)
            return SBT[n]

        def PSU(n, s):
            SBT[n] = es.enter_context(nc.psum_tensor(n, s, F32))
            return SBT[n]

        SBT.update(DRAMT)
        s_xt = [SB("s_xt0", [128, D]), SB("s_xt1", [128, D])]
        s_sum = SB("s_sum", [128, 16]); s_ssq = SB("s_ssq", [128, 16])
        s_m = SB("s_m", [128, 16]); s_var = SB("s_var", [128, 16])
        s_std = SB("s_std", [128, 16]); s_c1 = SB("s_c1", [128, 16])
        s_rmat = SB("s_rmat", [128, 16]); s_mrmat = SB("s_mrmat", [128, 16])
        s_xng = SB("s_xng", [128, 4 * D])
        s_xfm = SB("s_xfm", [128, 8 * 512])
        s_wqk = SB("s_wqk", [128, 8 * 128]); s_wvg = SB("s_wvg", [128, 8 * 72])
        s_bqk = SB("s_bqk", [128, 1]); s_bvg = SB("s_bvg", [72, 1])
        s_wo = SB("s_wo", [64, D])
        s_b1c = SB("s_b1c", [128, 32])
        s_b2bc = SB("s_b2bc", [128, D]); s_bobc = SB("s_bobc", [128, D])
        s_id = SB("s_id", [128, 128]); s_tri = SB("s_tri", [128, 128])
        s_dmat = SB("s_dmat", [128, 512]); s_dtbl = SB("s_dtbl", [128, NPAIR])
        s_m32 = SB("s_m32", [32, 4])
        s_one1 = SB("s_one1", [1, 128]); s_onecol = SB("s_onecol", [128, 1])
        s_iota8 = SB("s_iota8", [128, 8]); s_io64 = SB("s_io64", [128, 8])
        s_slotid = SB("s_slotid", [128, 32])
        s_kidx = SB("s_kidx", [128, 8], I32); s_vidx = SB("s_vidx", [128, 32], I32)
        s_tslice = SB("s_tslice", [128, NST], I32)
        s_tglob = SB("s_tglob", [128, 32])
        s_z = SB("s_z", [65, TPC])
        s_recip = SB("s_recip", [128, 16])
        s_vt = [SB("s_vt0", [128, 65]), SB("s_vt1", [128, 65])]
        s_logit = SB("s_logit", [128, 16 * 8])
        s_m8 = SB("s_m8", [128, 16 * 8])
        s_m1 = SB("s_m1", [128, 16]); s_m2 = SB("s_m2", [128, 16])
        s_m1n = SB("s_m1n", [128, 16]); s_m2n = SB("s_m2n", [128, 16])
        s_e1 = SB("s_e1", [128, 16]); s_e2 = SB("s_e2", [128, 16])
        s_g1 = SB("s_g1", [128, 16]); s_g2 = SB("s_g2", [128, 16])
        s_eq = SB("s_eq", [128, 8]); s_ta = SB("s_ta", [128, 8])
        s_le = SB("s_le", [128, 8]); s_excl = SB("s_excl", [128, 1])
        s_M = SB("s_M", [128, 256])
        s_R2 = SB("s_R2", [128, 128])
        s_trip = SB("s_trip", [128, 128])
        s_tot = SB("s_tot", [1, 128]); s_tots = SB("s_tots", [1, 128])
        s_tinc = SB("s_tinc", [1, 128])
        s_pos = SB("s_pos", [128, 256])
        s_cnt = SB("s_cnt", [1, 32])
        s_ccag = SB("s_ccag", [32, 8])
        s_bgl = SB("s_bgl", [4, 8]); s_bglr = SB("s_bglr", [1, 32])
        s_bglb = SB("s_bglb", [128, 32])
        s_esel = SB("s_esel", [128, 32]); s_psel = SB("s_psel", [128, 32])
        s_scr32 = SB("s_scr32", [128, 32])
        s_scr32b = SB("s_scr32b", [128, 32])
        s_valid = SB("s_valid", [128, 32])
        s_sa = SB("s_sa", [128, 32]); s_sb = SB("s_sb", [128, 32])
        s_gsel = SB("s_gsel", [128, 32])
        s_dest = SB("s_dest", [128, 32], I32)
        s_yidxf = SB("s_yidxf", [128, 32])
        s_yidx = SB("s_yidx", [128, 32], I32)
        s_payl = SB("s_payl", [128, 128])
        s_zero = SB("s_zero", [128, 512])
        s_cst = SB("s_cst", [128, 2])
        nc.const_aps.register(F32, 1e-5, s_cst[:, 0:1])
        nc.const_aps.register(F32, 0.125, s_cst[:, 1:2])
        s_trow = SB("s_trow", [128, NST * 4])
        s_tcast = SB("s_tcast", [128, NST], I32)
        s_xg = [SB(f"s_xg{i}", [128, D]) for i in range(4)]
        s_w1hl = SB("s_w1hl", [128, 2 * 4096], mybir.dt.bfloat16)
        s_w2hl = SB("s_w2hl", [128, 2 * 1024], mybir.dt.bfloat16)
        s_yev = [SB("s_yev0", [128, 512]), SB("s_yev1", [128, 512])]

        s_zo = SB("s_zo", [128, D])
        s_xb = SB("s_xb", [128, D])
        s_spf = [SB("s_spf0", [128, 512]), SB("s_spf1", [128, 512])]
        s_sph = [SB("s_sph0", [128, 512], mybir.dt.bfloat16),
                 SB("s_sph1", [128, 512], mybir.dt.bfloat16)]
        s_spl = [SB("s_spl0", [128, 512], mybir.dt.bfloat16),
                 SB("s_spl1", [128, 512], mybir.dt.bfloat16)]
        AB = (_cursor[0] + 63) // 64 * 64
        s_kallF = SB("s_kallF", [128, S], at=AB + 0)
        s_kall = s_kallF
        s_vall = SB("s_vall", [128, 32 * 65], at=AB + 16384)
        s_mask = SB("s_mask", [128, 512], at=AB + 24704)
        s_u = [SB(f"s_u{i}", [128, 512], at=AB + 26752 + 2048 * i)
               for i in range(4)]
        s_pt = [SB(f"s_pt{i}", [128, 512], at=AB + 34944 + 2048 * i)
                for i in range(4)]
        s_qF = SB("s_qF", [128, TPC], at=AB + 43136)
        s_q = s_qF
        s_sq = SB("s_sq", [128, D], at=AB + 0)
        s_kev = SB("s_kev", [64, 2 * 512], at=AB + 55424)
        s_vgev = SB("s_vgev", [72, 2 * 512], at=AB + 59520)
        s_Mp = SB("s_Mp", [128, 256], at=AB + 63616)
        s_pexc = SB("s_pexc", [128, 128], at=AB + 64640)
        s_hh = SB("s_hh", [128, 32 * 512], mybir.dt.bfloat16, at=AB + 0)
        s_hl = SB("s_hl", [128, 32 * 512], mybir.dt.bfloat16, at=AB + 32768)
        s_xgfh = SB("s_xgfh", [128, 8 * 512], mybir.dt.bfloat16,
                    at=SBOFF["s_xfm"])
        s_xgfl = SB("s_xgfl", [128, 8 * 512], mybir.dt.bfloat16,
                    at=SBOFF["s_xfm"] + 8192)
        s_gel = SB("s_gel", [128, 512], at=SBOFF["s_zero"])
        s_yg0 = [SB("s_yg0a", [128, D], at=AB + 0),
                 SB("s_yg0b", [128, D], at=AB + 24576)]
        s_yg1 = [SB("s_yg1a", [128, D], at=AB + 4096),
                 SB("s_yg1b", [128, D], at=AB + 28672)]

        s_acc = SB("s_acc", [128, D], at=AB + 16384)
        s_out = SB("s_out", [128, D], at=AB + 20480)
        assert AB + 65536 <= 229376, f"SBUF over budget: {AB + 65536}"

        pb = [PSU(f"pb{i}", [128, 512]) for i in range(8)]

        for nm in ["dmaA", "dmaB", "ld", "ld2", "tp", "fm", "mm", "vr", "sr",
                   "kvkA", "kvkB", "kvvA", "kvvB",
                   "cck", "ccv", "ccc", "cct", "ccy0", "ccy1", "ccy2", "ccy3", "ccy4", "ccy5",
                   "gz1", "gz2", "gk", "gv", "gt", "gs", "gc",
                   "gx0", "gx1", "gx2", "gx3", "gq",
                   "gy0a", "gy0b", "gy1a", "gy1b",
                   "w1a", "w1b", "w2a", "w2b", "yya", "yyb", "oo",
                   "sl0", "sl1", "sh0", "sh1"]:
            P.sem(nm)

        EV = {}

        def ld(dst, src):
            return P.op("sync", lambda e, d=dst, s=src: e.dma_start(out=d, in_=s),
                        inc="ld", inck=16, dma=True)

        for dcc in range(8):
            EV[f"wqk{dcc}"] = ld(s_wqk[:, dcc * 128:(dcc + 1) * 128],
                                 DI["wqk"][dcc * 128:(dcc + 1) * 128, :])
            EV[f"wvg{dcc}"] = ld(s_wvg[:, dcc * 72:(dcc + 1) * 72],
                                 DI["wvg"][dcc * 128:(dcc + 1) * 128, :])
        EV["kidx"] = ld(s_kidx[0:64, :], DI["kidx"][:, :])
        EV["kidx_b"] = ld(s_kidx[64:128, :], DI["kidx"][:, :])
        for n1, t1 in [("bqk", s_bqk), ("bvg", s_bvg), ("wo", s_wo),
                       ("b1c", s_b1c), ("ident", s_id), ("tri", s_tri),
                       ("dmat", s_dmat), ("dtbl", s_dtbl), ("m32", s_m32),
                       ("vidx", s_vidx), ("tslice", s_tslice),
                       ("tglob", s_tglob)]:
            EV[n1] = ld(t1[:], DI[n1][:, :])
        EV["b2row"] = ld(s_b2bc[0:1, :], DI["b2row"][:, :])
        EV["bo_row"] = ld(s_bobc[0:1, :], DI["bo_row"][:, :])
        _ld_last = EV["bo_row"]
        for _k in list(EV.keys()):
            EV[_k] = _ld_last

        ev_io = P.op("gpsimd", lambda e: e.iota(
            s_iota8[:], pattern=[[1, 8]], base=0, channel_multiplier=0,
            allow_small_or_imprecise_dtypes=True), inc="gc")
        ev_sid = P.op("gpsimd", lambda e: e.iota(
            s_slotid[:], pattern=[[1, 32]], base=0, channel_multiplier=32,
            allow_small_or_imprecise_dtypes=True), inc="gc")
        ev_o1 = P.op("gpsimd", lambda e: e.memset(s_one1[:], 1.0), inc="gc")
        ev_oc = P.op("gpsimd", lambda e: e.memset(s_onecol[:], 1.0), inc="gc")
        ev_zz = P.op("gpsimd", lambda e: e.memset(s_zero[:], 0.0), inc="gc")
        ev_ceps = P.op("gpsimd", lambda e: e.memset(s_cst[:, 0:1], 1e-5), inc="gc")
        ev_c125 = P.op("gpsimd", lambda e: e.memset(s_cst[:, 1:2], 0.125), inc="gc")
        ev_i64 = P.op("vector", lambda e: e.tensor_scalar(
            s_io64[:], s_iota8[:], -64.0, None, ALU.add), waits=[ev_io], inc="vr")

        tz = tbl_in.ap().rearrange("(a b) c -> a (b c)", a=128)
        ev_tz = []
        for j in range(3):
            w = min(512, NTBL * 4 // 128 - j * 512)
            ev_tz.append(P.op("gpsimd", lambda e, j=j, w=w: e.dma_start(
                out=tz[:, j * 512:j * 512 + w], in_=s_zero[:, 0:w]),
                waits=[ev_zz], inc="gz1", inck=16, dma=True))
        ev_yz = []
        for half in range(2):
            ev_yz.append(P.op("gpsimd", lambda e, h=half: e.dma_start(
                out=y_in.ap()[ZROW:YROWS, h * 512:(h + 1) * 512],
                in_=s_zero[0:YROWS - ZROW, 0:512]),
                waits=[ev_zz], inc="gz1", inck=16, dma=True))

        prev_bc = None
        for (dst, src_ev) in [(s_b2bc, EV["b2row"]), (s_bobc, EV["bo_row"])]:
            for dgi in range(2):
                evm = P.op("tensor", lambda e, d=dst, g=dgi: e.matmul(
                    pb[6][:, 0:512], s_one1[:], d[0:1, g * 512:(g + 1) * 512],
                    start=True, stop=True),
                    waits=[src_ev, ev_o1, prev_bc], inc="mm")
                prev_bc = P.op("vector", lambda e, d=dst, g=dgi: e.tensor_copy(
                    d[:, g * 512:(g + 1) * 512], pb[6][:, 0:512]),
                    waits=[evm], inc="vr")
        ev_bcast_done = prev_bc

        split_state = {"ld": [None, None], "cast": [None, None],
                       "st": [None, None], "i": 0}
        SPLIT_JOBS = []
        SPLIT_EV = {}
        for c0 in range(0, HD, 512):
            for r0 in range(0, D, 128):
                SPLIT_JOBS.append((0, DI["w1"], r0, c0))
        for r0 in range(0, HD, 128):
            for c0 in range(0, D, 512):
                SPLIT_JOBS.append((1, DI["w2"], r0, c0))

        def _split_dst(wi, r0, c0, hilo):
            if wi == 0:
                v = w1s.ap().rearrange("p (d g h c) -> p d g h c",
                                       d=8, g=16, h=2, c=256)
                dcc, g0 = r0 // 128, c0 // 256
                return v[:, dcc:dcc + 1, g0:g0 + 2, hilo:hilo + 1, :]
            v = w2s.ap().rearrange("p (t g h c) -> p t g h c",
                                   t=32, g=2, h=2, c=512)
            return v[:, r0 // 128:r0 // 128 + 1, c0 // 512:c0 // 512 + 1,
                     hilo:hilo + 1, :]

        def emit_split(n):
            st = split_state
            for _ in range(n):
                i = st["i"]
                if i >= len(SPLIT_JOBS):
                    return
                st["i"] += 1
                wi, wsrc, r0, c0 = SPLIT_JOBS[i]
                b = i % 2
                evl = P.op("scalar", lambda e, b=b, wsrc=wsrc, r0=r0, c0=c0:
                           e.dma_start(out=s_spf[b][:],
                                       in_=wsrc[r0:r0 + 128, c0:c0 + 512]),
                           waits=[st["cast"][b], st["ld"][b]],
                           inc=f"sl{b}", inck=16, dma=True)
                st["ld"][b] = evl
                evh = P.op("vector", lambda e, b=b: e.tensor_copy(
                    s_sph[b][:], s_spf[b][:]),
                    waits=[evl, st["st"][b]], inc="vr")
                evc = P.op("vector", lambda e, b=b: e.tensor_tensor(
                    s_spl[b][:], s_spf[b][:], s_sph[b][:], ALU.subtract),
                    waits=[evh], inc="vr")
                st["cast"][b] = evc
                evs1 = P.op("gpsimd", lambda e, b=b, wi=wi, r0=r0, c0=c0:
                            e.dma_start(out=_split_dst(wi, r0, c0, 0),
                                        in_=s_sph[b][:]),
                            waits=[evc, st["st"][b]],
                            inc=f"sh{b}", inck=16, dma=True)
                evs2 = P.op("gpsimd", lambda e, b=b, wi=wi, r0=r0, c0=c0:
                            e.dma_start(out=_split_dst(wi, r0, c0, 1),
                                        in_=s_spl[b][:]),
                            waits=[evc, evs1], inc=f"sh{b}", inck=16, dma=True)
                st["st"][b] = evs2
                SPLIT_EV[(wi, r0, c0)] = evs2
            return

        def split_done_ev():
            return [split_state["st"][0], split_state["st"][1]]

        ev_xn = [None] * 16
        ev_sq_prev = [None]
        ev_xload_prev = [None, None]
        ev_tile_tp = [None] * 16
        ev_logit = [None] * 16
        ev_r_all = [None] * 16
        ev_mr_all = [None] * 16
        ev_kdma = []
        ev_vdma = [None] * 16
        ev_qev_all = [None] * 4
        ev_qb = []
        prev_fm = {}
        ev_qk_free = None
        ev_vg_free = None
        ev_vge_used = [None, None]
        ev_kev_used = [None, None]

        for g5 in range(4):
            g4 = slice(g5 * 4, g5 * 4 + 4)
            ev_red = []
            for ti in range(4):
                tt = g5 * 4 + ti
                evx = P.op("sync", lambda e, tt=tt: e.dma_start(
                    out=s_xng[:, (tt % 4) * D:(tt % 4 + 1) * D],
                    in_=DI["x_own"][tt * 128:(tt + 1) * 128, :]),
                    waits=[ev_tile_tp[tt - 4] if tt >= 4 else None,
                           ev_xload_prev[tt % 2]],
                    inc="dmaA" if tt % 2 == 0 else "dmaB",
                    inck=16, dma=True)
                ev_xload_prev[tt % 2] = evx
                ev_red.append(P.op("vector", lambda e, tt=tt: e.tensor_reduce(
                    s_sum[:, tt:tt + 1], s_xng[:, (tt % 4) * D:(tt % 4 + 1) * D],
                    AX.X, ALU.add), waits=[evx], inc="vr"))
                ev_sq_prev[0] = P.op("scalar", lambda e, tt=tt: e.activation(
                    out=s_sq[:], in_=s_xng[:, (tt % 4) * D:(tt % 4 + 1) * D],
                    func=AF.Square, bias=0.0, scale=1.0,
                    accum_out=s_ssq[:, tt:tt + 1]),
                    waits=[evx, ev_sq_prev[0]], inc="sr")
                ev_red.append(ev_sq_prev[0])
                emit_split(4)
            ev_m = P.op("vector", lambda e, g4=g4: e.tensor_scalar(
                s_m[:, g4], s_sum[:, g4], 1.0 / D, None, ALU.mult),
                waits=ev_red, inc="vr")
            ev_msq = P.op("vector", lambda e, g4=g4: e.tensor_tensor(
                s_c1[:, g4], s_m[:, g4], s_m[:, g4], ALU.mult),
                waits=[ev_m], inc="vr")
            ev_ms = P.op("vector", lambda e, g4=g4: e.tensor_scalar(
                s_var[:, g4], s_ssq[:, g4], 1.0 / D, None, ALU.mult),
                waits=ev_red, inc="vr")
            ev_var = P.op("vector", lambda e, g4=g4: e.tensor_tensor(
                s_var[:, g4], s_var[:, g4], s_c1[:, g4], ALU.subtract),
                waits=[ev_msq, ev_ms], inc="vr")
            ev_std = P.op("scalar", lambda e, g4=g4: e.activation(
                out=s_std[:, g4], in_=s_var[:, g4], func=AF.Sqrt,
                bias=1e-5, scale=1.0), waits=[ev_var, ev_ceps], inc="sr")
            ev_r = P.op("vector", lambda e, g4=g4: e.reciprocal(
                s_rmat[:, g4], s_std[:, g4]), waits=[ev_std], inc="vr")
            ev_mr0 = P.op("vector", lambda e, g4=g4: e.tensor_tensor(
                s_c1[:, g4], s_m[:, g4], s_rmat[:, g4], ALU.mult),
                waits=[ev_r], inc="vr")
            ev_mr = P.op("vector", lambda e, g4=g4: e.tensor_scalar(
                s_mrmat[:, g4], s_c1[:, g4], -1.0, None, ALU.mult),
                waits=[ev_mr0], inc="vr")
            for ti in range(4):
                tt = g5 * 4 + ti
                ev_r_all[tt] = ev_r
                ev_mr_all[tt] = ev_mr
                ev_xn[tt] = P.op("scalar", lambda e, tt=tt: e.activation(
                    out=s_xng[:, (tt % 4) * D:(tt % 4 + 1) * D],
                    in_=s_xng[:, (tt % 4) * D:(tt % 4 + 1) * D],
                    func=AF.Identity, bias=s_mrmat[:, tt:tt + 1],
                    scale=s_rmat[:, tt:tt + 1]),
                    waits=[ev_mr], inc="sr")

            evfm_all = []
            for ti in range(4):
                tt = g5 * 4 + ti
                for dcc in range(8):
                    bank = (ti * 8 + dcc) % 2
                    evt = P.op("tensor", lambda e, tt=tt, dcc=dcc, bank=bank:
                               e.transpose(
                        pb[bank][:, 0:128],
                        s_xng[:, (tt % 4) * D + dcc * 128:(tt % 4) * D + (dcc + 1) * 128],
                        s_id[:]),
                        waits=[ev_xn[tt], EV["ident"], prev_fm.get(bank)],
                        inc="tp")
                    prev_fm[bank] = P.op(
                        "vector", lambda e, ti=ti, dcc=dcc, bank=bank: e.tensor_copy(
                            s_xfm[:, dcc * 512 + ti * 128: dcc * 512 + (ti + 1) * 128],
                            pb[bank][:, 0:128]), waits=[evt], inc="fm")
                    evfm_all.append(prev_fm[bank])
                ev_tile_tp[tt] = evfm_all[-1]
            last = None
            for dcc in range(8):
                last = P.op("tensor", lambda e, dcc=dcc: e.matmul(
                    pb[2][:, 0:512], s_wqk[:, dcc * 128:(dcc + 1) * 128],
                    s_xfm[:, dcc * 512:(dcc + 1) * 512],
                    start=(dcc == 0), stop=(dcc == 7)),
                    waits=(evfm_all + [EV[f"wqk{dcc}"], ev_qk_free]
                           if dcc == 0 else [EV[f"wqk{dcc}"]]),
                    inc="mm" if dcc == 7 else None)
            ev_qkmm = last
            for dcc in range(8):
                last = P.op("tensor", lambda e, dcc=dcc: e.matmul(
                    pb[3][0:72, 0:512], s_wvg[:, dcc * 72:(dcc + 1) * 72],
                    s_xfm[:, dcc * 512:(dcc + 1) * 512],
                    start=(dcc == 0), stop=(dcc == 7)),
                    waits=([EV[f"wvg{dcc}"], ev_vg_free] if dcc == 0
                           else [EV[f"wvg{dcc}"]]),
                    inc="mm" if dcc == 7 else None)
            ev_vgmm = last
            ev_qev = P.op("scalar", lambda e, g5=g5: e.activation(
                out=s_q[0:64, g5 * 512:(g5 + 1) * 512], in_=pb[2][0:64, 0:512],
                func=AF.Identity, bias=s_bqk[0:64, 0:1], scale=1.0),
                waits=[ev_qkmm, EV["bqk"]], inc="sr")
            ev_qev_all[g5] = ev_qev
            ev_kev = P.op("scalar", lambda e, g5=g5: e.activation(
                out=s_kev[:, (g5 % 2) * 512:(g5 % 2 + 1) * 512],
                in_=pb[2][64:128, 0:512], func=AF.Identity,
                bias=s_bqk[64:128, 0:1], scale=1.0),
                waits=[ev_qkmm, ev_kev_used[g5 % 2]], inc="sr")
            ev_qk_free = ev_kev
            ev_vge = P.op("scalar", lambda e, g5=g5: e.activation(
                out=s_vgev[:, (g5 % 2) * 512:(g5 % 2 + 1) * 512],
                in_=pb[3][0:72, 0:512], func=AF.Identity, bias=s_bvg[:, 0:1],
                scale=1.0), waits=[ev_vgmm, EV["bvg"], ev_vge_used[g5 % 2]],
                inc="sr")
            ev_vg_free = ev_vge
            kvkv = kv_in.ap()[0:64 * 2048].rearrange("(h w) -> h w", h=64)
            ev_kd = P.op("scalar", lambda e, g5=g5, kvkv=kvkv: e.dma_start(
                out=kvkv[:, g5 * 512:(g5 + 1) * 512],
                in_=s_kev[:, (g5 % 2) * 512:(g5 % 2 + 1) * 512]),
                waits=[ev_kev], inc="kvkA" if g5 % 2 == 0 else "kvkB",
                inck=16, dma=True)
            ev_kdma.append(ev_kd)
            ev_kev_used[g5 % 2] = ev_kd
            last_vt = None
            for ti in range(4):
                tt = g5 * 4 + ti
                evtp = P.op("tensor", lambda e, g5=g5, ti=ti: e.transpose(
                    pb[4][0:128, 0:72],
                    s_vgev[:, (g5 % 2) * 512 + ti * 128:(g5 % 2) * 512 + (ti + 1) * 128],
                    s_id[0:72, 0:72]),
                    waits=[ev_vge, last_vt], inc="tp")
                evc1 = P.op("vector", lambda e, tt=tt: e.tensor_copy(
                    s_vt[tt % 2][:, 0:64], pb[4][:, 0:64]),
                    waits=[evtp, ev_vdma[tt - 2] if tt >= 2 else None], inc="fm")
                evc2 = P.op("vector", lambda e, tt=tt: e.memset(
                    s_vt[tt % 2][:, 64:65], 1.0),
                    waits=[ev_vdma[tt - 2] if tt >= 2 else None], inc="fm")
                evc3 = P.op("vector", lambda e, tt=tt: e.tensor_copy(
                    s_logit[:, tt * 8:(tt + 1) * 8], pb[4][:, 64:72]),
                    waits=[evtp], inc="fm")
                ev_logit[tt] = evc3
                last_vt = evc3
                kvvv = kv_in.ap()[64 * 2048:KVLEN].rearrange("(a w) -> a w", w=65)
                ev_vdma[tt] = P.op("gpsimd", lambda e, tt=tt, kvvv=kvvv: e.dma_start(
                    out=kvvv[tt * 128:(tt + 1) * 128, :], in_=s_vt[tt % 2][:]),
                    waits=[evc1, evc2], inc="kvvA" if tt % 2 == 0 else "kvvB",
                    inck=16, dma=True)
            ev_vge_used[g5 % 2] = last_vt
            ev_qb.append(P.op("gpsimd", lambda e, g5=g5: e.dma_start(
                out=s_qF[64:128, g5 * 512:(g5 + 1) * 512],
                in_=s_qF[0:64, g5 * 512:(g5 + 1) * 512]),
                waits=[ev_qev], inc="gq", inck=16, dma=True))

        ev_agkv = P.op("gpsimd", lambda e: e.collective_compute(
            "AllGather", ALU.bypass, replica_groups=[list(range(8))],
            ins=[kv_in.ap().opt()], outs=[kv_all.ap().opt()]),
            waits=ev_kdma + list(ev_vdma), inc="cck", dma=True)

        if stop_after == "phaseA":
            _finish(P, DBG, debug_taps, SBT, y_out, [ev_agkv])
            _emit(P, nc, es)
            return nc

        ev_top = [None] * 16
        for tt in range(16):
            evmax = P.op("vector", lambda e, tt=tt: e.max(
                s_m8[:, tt * 8:(tt + 1) * 8], s_logit[:, tt * 8:(tt + 1) * 8]),
                waits=[ev_logit[tt]], inc="vr")
            ev1 = P.op("vector", lambda e, tt=tt: e.tensor_copy(
                s_m1[:, tt:tt + 1], s_m8[:, tt * 8:tt * 8 + 1]),
                waits=[evmax], inc="vr")
            ev2 = P.op("vector", lambda e, tt=tt: e.tensor_copy(
                s_m2[:, tt:tt + 1], s_m8[:, tt * 8 + 1:tt * 8 + 2]),
                waits=[evmax], inc="vr")
            eveq = P.op("vector", lambda e, tt=tt: e.tensor_scalar(
                s_eq[:], s_logit[:, tt * 8:(tt + 1) * 8], s_m1[:, tt:tt + 1],
                None, ALU.is_equal), waits=[ev1], inc="vr")
            evta = P.op("vector", lambda e: e.tensor_tensor(
                s_ta[:], s_eq[:], s_io64[:], ALU.mult),
                waits=[eveq, ev_i64], inc="vr")
            evtb = P.op("vector", lambda e: e.tensor_scalar(
                s_ta[:], s_ta[:], 64.0, None, ALU.add), waits=[evta], inc="vr")
            eve1 = P.op("vector", lambda e, tt=tt: e.tensor_reduce(
                s_e1[:, tt:tt + 1], s_ta[:], AX.X, ALU.min),
                waits=[evtb], inc="vr")
            eveq2 = P.op("vector", lambda e, tt=tt: e.tensor_scalar(
                s_eq[:], s_logit[:, tt * 8:(tt + 1) * 8], s_m2[:, tt:tt + 1],
                None, ALU.is_equal), waits=[ev2, eve1], inc="vr")
            evtc = P.op("vector", lambda e: e.tensor_tensor(
                s_ta[:], s_eq[:], s_io64[:], ALU.mult), waits=[eveq2], inc="vr")
            evtd = P.op("vector", lambda e: e.tensor_scalar(
                s_ta[:], s_ta[:], 64.0, None, ALU.add), waits=[evtc], inc="vr")
            evex = P.op("vector", lambda e, tt=tt: e.tensor_scalar(
                s_excl[:], s_m2[:, tt:tt + 1], s_m1[:, tt:tt + 1], None,
                ALU.is_equal), waits=[ev1, ev2], inc="vr")
            evle = P.op("vector", lambda e, tt=tt: e.tensor_scalar(
                s_le[:], s_iota8[:], s_e1[:, tt:tt + 1], None, ALU.is_le),
                waits=[eve1, ev_io], inc="vr")
            evle2 = P.op("vector", lambda e: e.tensor_scalar(
                s_le[:], s_le[:], s_excl[:, 0:1], 1000.0, ALU.mult, ALU.mult),
                waits=[evle, evex], inc="vr")
            evte = P.op("vector", lambda e: e.tensor_tensor(
                s_ta[:], s_ta[:], s_le[:], ALU.add), waits=[evtd, evle2], inc="vr")
            eve2 = P.op("vector", lambda e, tt=tt: e.tensor_reduce(
                s_e2[:, tt:tt + 1], s_ta[:], AX.X, ALU.min),
                waits=[evte], inc="vr")
            evm1n = P.op("vector", lambda e, tt=tt: e.tensor_scalar(
                s_m1n[:, tt:tt + 1], s_m1[:, tt:tt + 1], -1.0, None, ALU.mult),
                waits=[ev1], inc="vr")
            evm2n = P.op("vector", lambda e, tt=tt: e.tensor_scalar(
                s_m2n[:, tt:tt + 1], s_m2[:, tt:tt + 1], -1.0, None, ALU.mult),
                waits=[ev2], inc="vr")
            evg1 = P.op("scalar", lambda e, tt=tt: e.activation(
                out=s_g1[:, tt:tt + 1], in_=s_m1[:, tt:tt + 1], func=AF.Sigmoid,
                bias=s_m2n[:, tt:tt + 1], scale=1.0), waits=[evm2n], inc="sr")
            evg2 = P.op("scalar", lambda e, tt=tt: e.activation(
                out=s_g2[:, tt:tt + 1], in_=s_m2[:, tt:tt + 1], func=AF.Sigmoid,
                bias=s_m1n[:, tt:tt + 1], scale=1.0), waits=[evm1n], inc="sr")
            ev_top[tt] = (eve1, eve2, evg1, evg2)

        all_top = [x for t4 in ev_top for x in t4]

        def rv(tile, pat, **kw):
            return tile[:].rearrange(pat, **kw)

        ev_M = []
        for e8 in range(E):
            for k in range(2):
                src = s_e1 if k == 0 else s_e2
                ev_M.append(P.op("vector", lambda e, e8=e8, k=k, src=src:
                                 e.tensor_scalar(
                    rv(s_M, "p (c e t k) -> p c e t k", c=4, e=8, t=4)[
                        :, :, e8:e8 + 1, :, k:k + 1],
                    rv(src, "p (c o t u) -> p c o t u", c=4, o=1, u=1),
                    float(e8), None, ALU.is_equal),
                    waits=(all_top if (e8 == 0 and k == 0) else []), inc="vr"))
        ev_R2 = P.op("vector", lambda e: e.tensor_tensor(
            s_R2[:],
            rv(s_M, "p (a k) -> p a k", k=2)[:, :, 0:1],
            rv(s_M, "p (a k) -> p a k", k=2)[:, :, 1:2], ALU.add),
            waits=ev_M, inc="vr")
        ev_trm = P.op("tensor", lambda e: e.matmul(
            pb[5][:, 0:128], s_tri[:], s_R2[:], start=True, stop=True),
            waits=[ev_R2, EV["tri"]], inc="mm")
        ev_tripv = P.op("vector", lambda e: e.tensor_copy(s_trip[:], pb[5][:, 0:128]),
                        waits=[ev_trm], inc="vr")
        ev_totm = P.op("tensor", lambda e: e.matmul(
            pb[5][0:1, 0:128], s_onecol[:], s_R2[:], start=True, stop=True),
            waits=[ev_R2, ev_oc, ev_tripv], inc="mm")
        ev_totv = P.op("vector", lambda e: e.tensor_copy(s_tot[:], pb[5][0:1, 0:128]),
                       waits=[ev_totm], inc="vr")
        ev_p1 = P.op("vector", lambda e: e.tensor_copy(s_tinc[:], s_tot[:]),
                     waits=[ev_totv], inc="vr")
        ev_p2 = P.op("vector", lambda e: e.tensor_tensor(
            rv(s_tinc, "p (a t) -> p a t", t=4)[:, :, 1:4],
            rv(s_tot, "p (a t) -> p a t", t=4)[:, :, 1:4],
            rv(s_tot, "p (a t) -> p a t", t=4)[:, :, 0:3], ALU.add),
            waits=[ev_p1], inc="vr")
        ev_p3 = P.op("vector", lambda e: e.tensor_tensor(
            rv(s_tots, "p (a t) -> p a t", t=4)[:, :, 2:4],
            rv(s_tinc, "p (a t) -> p a t", t=4)[:, :, 2:4],
            rv(s_tinc, "p (a t) -> p a t", t=4)[:, :, 0:2], ALU.add),
            waits=[ev_p2], inc="vr")
        ev_p4 = P.op("vector", lambda e: e.tensor_copy(
            rv(s_tots, "p (a t) -> p a t", t=4)[:, :, 0:2],
            rv(s_tinc, "p (a t) -> p a t", t=4)[:, :, 0:2]),
            waits=[ev_p2], inc="vr")
        ev_p5 = P.op("vector", lambda e: e.tensor_tensor(
            s_tots[:], s_tots[:], s_tot[:], ALU.subtract),
            waits=[ev_p3, ev_p4], inc="vr")
        ev_bcm = P.op("tensor", lambda e: e.matmul(
            pb[5][:, 0:128], s_one1[:], s_tots[:], start=True, stop=True),
            waits=[ev_p5, ev_totv], inc="mm")
        ev_pexc = P.op("vector", lambda e: e.tensor_copy(s_pexc[:], pb[5][:, 0:128]),
                       waits=[ev_bcm], inc="vr")
        ev_cn = P.op("vector", lambda e: e.tensor_reduce(
            s_cnt[:], rv(s_tot, "p (a t) -> p a t", t=4), AX.X, ALU.add),
            waits=[ev_totv], inc="vr")
        ev_cd = P.op("gpsimd", lambda e: e.dma_start(
            out=cc_cnt_in.ap()[:, :], in_=s_cnt[:]),
            waits=[ev_cn], inc="gz2", inck=16, dma=True)
        ev_agc = P.op("gpsimd", lambda e: e.collective_compute(
            "AllGather", ALU.bypass, replica_groups=[list(range(8))],
            ins=[cc_cnt_in.ap().opt()], outs=[cc_cnt_out.ap().opt()]),
            waits=[ev_cd], inc="ccc", dma=True)
        ev_ccl = P.op("gpsimd", lambda e: e.dma_start(
            out=s_ccag[:],
            in_=cc_cnt_out.ap().rearrange("r (c e) -> (r c) e", e=8)),
            waits=[ev_agc], inc="ld2", inck=16, dma=True)
        ev_bglm = P.op("tensor", lambda e: e.matmul(
            pb[6][0:4, 0:8], s_m32[:], s_ccag[:], start=True, stop=True),
            waits=[ev_ccl, EV["m32"], ev_bcast_done], inc="mm")
        ev_bglv = P.op("vector", lambda e: e.tensor_copy(s_bgl[:], pb[6][0:4, 0:8]),
                       waits=[ev_bglm], inc="vr")
        ev_bgd = P.op("gpsimd", lambda e: e.dma_start(
            out=bgl_scr.ap().rearrange("a (b c) -> (a b) c", c=8), in_=s_bgl[:]),
            waits=[ev_bglv], inc="gz2", inck=16, dma=True)
        ev_bgl2 = P.op("gpsimd", lambda e: e.dma_start(
            out=s_bglr[:], in_=bgl_scr.ap()[:, :]),
            waits=[ev_bgd], inc="ld2", inck=16, dma=True)
        ev_bglm2 = P.op("tensor", lambda e: e.matmul(
            pb[6][:, 0:32], s_one1[:], s_bglr[:], start=True, stop=True),
            waits=[ev_bgl2, ev_bglv], inc="mm")
        ev_bglbv = P.op("vector", lambda e: e.tensor_copy(s_bglb[:], pb[6][:, 0:32]),
                        waits=[ev_bglm2], inc="vr")
        ev_pk0 = P.op("vector", lambda e: e.tensor_tensor(
            rv(s_pos, "p (a k) -> p a k", k=2)[:, :, 0:1],
            rv(s_trip, "p (a o) -> p a o", o=1),
            rv(s_pexc, "p (a o) -> p a o", o=1), ALU.add),
            waits=[ev_tripv, ev_pexc], inc="vr")
        ev_pk1 = P.op("vector", lambda e: e.tensor_tensor(
            rv(s_pos, "p (a k) -> p a k", k=2)[:, :, 1:2],
            rv(s_pos, "p (a k) -> p a k", k=2)[:, :, 0:1],
            rv(s_M, "p (a k) -> p a k", k=2)[:, :, 0:1], ALU.add),
            waits=[ev_pk0], inc="vr")
        ev_pg = []
        for t4 in range(4):
            for k in range(2):
                ev_pg.append(P.op("vector", lambda e, t4=t4, k=k: e.tensor_tensor(
                    rv(s_pos, "p (c e t k) -> p c e t k", c=4, e=8, t=4)[
                        :, :, :, t4:t4 + 1, k:k + 1],
                    rv(s_pos, "p (c e t k) -> p c e t k", c=4, e=8, t=4)[
                        :, :, :, t4:t4 + 1, k:k + 1],
                    rv(s_bglb, "p (c e o u) -> p c e o u", c=4, o=1, u=1),
                    ALU.add),
                    waits=[ev_pk1, ev_bglbv], inc="vr"))

        def interleave(dst, srck0, srck1, waits):
            e0 = P.op("vector", lambda e: e.tensor_copy(
                rv(dst, "p (c t k) -> p c t k", c=4, k=2)[:, :, :, 0:1],
                rv(srck0, "p (c t u) -> p c t u", c=4, u=1)), waits=waits, inc="vr")
            e1 = P.op("vector", lambda e: e.tensor_copy(
                rv(dst, "p (c t k) -> p c t k", c=4, k=2)[:, :, :, 1:2],
                rv(srck1, "p (c t u) -> p c t u", c=4, u=1)), waits=waits, inc="vr")
            return [e0, e1]

        ev_es = interleave(s_esel, s_e1, s_e2, all_top)
        ev_gs = interleave(s_gsel, s_g1, s_g2, all_top)
        ev_rs = interleave(s_sa, s_rmat, s_rmat, list(ev_r_all))
        ev_ms = interleave(s_sb, s_mrmat, s_mrmat, list(ev_mr_all))
        ev_mp = P.op("vector", lambda e: e.tensor_tensor(
            s_Mp[:], s_M[:], s_pos[:], ALU.mult), waits=ev_pg + ev_M, inc="vr")
        ev_ps = P.op("vector", lambda e: e.tensor_reduce(
            rv(s_psel, "p (c tk) -> p c tk", c=4),
            rv(s_Mp, "p (c e tk) -> p c tk e", c=4, e=8), AX.X, ALU.add),
            waits=[ev_mp], inc="vr")
        ev_vd = P.op("vector", lambda e: e.tensor_scalar(
            s_valid[:], s_psel[:], float(CAP), None, ALU.is_lt),
            waits=[ev_ps], inc="vr")
        pay = rv(s_payl, "p (s f) -> p s f", f=4)
        ev_pay = [
            P.op("vector", lambda e: e.tensor_copy(
                pay[:, :, 0:1], rv(s_tglob, "p (s u) -> p s u", u=1)),
                waits=[EV["tglob"]], inc="vr"),
            P.op("vector", lambda e: e.tensor_copy(
                pay[:, :, 1:2], rv(s_gsel, "p (s u) -> p s u", u=1)),
                waits=ev_gs, inc="vr"),
            P.op("vector", lambda e: e.tensor_copy(
                pay[:, :, 2:3], rv(s_sa, "p (s u) -> p s u", u=1)),
                waits=ev_rs, inc="vr"),
            P.op("vector", lambda e: e.tensor_copy(
                pay[:, :, 3:4], rv(s_sb, "p (s u) -> p s u", u=1)),
                waits=ev_ms, inc="vr"),
        ]
        ev_a1 = P.op("vector", lambda e: e.tensor_scalar(
            s_sa[:], s_esel[:], float(CAPP), None, ALU.mult),
            waits=ev_es + ev_pay, inc="vr")
        ev_a2 = P.op("vector", lambda e: e.tensor_tensor(
            s_sa[:], s_sa[:], s_psel[:], ALU.add), waits=[ev_a1, ev_ps], inc="vr")
        ev_b1 = P.op("vector", lambda e: e.tensor_scalar(
            s_sb[:], s_slotid[:], float(DUMP0), None, ALU.add),
            waits=[ev_sid] + ev_pay, inc="vr")
        ev_d1 = P.op("vector", lambda e: e.tensor_tensor(
            s_sa[:], s_sa[:], s_sb[:], ALU.subtract), waits=[ev_a2, ev_b1], inc="vr")
        ev_d2 = P.op("vector", lambda e: e.tensor_tensor(
            s_sa[:], s_sa[:], s_valid[:], ALU.mult), waits=[ev_d1, ev_vd], inc="vr")
        ev_d3 = P.op("vector", lambda e: e.tensor_tensor(
            s_sa[:], s_sa[:], s_sb[:], ALU.add), waits=[ev_d2], inc="vr")
        ev_dest = P.op("vector", lambda e: e.tensor_copy(s_dest[:], s_sa[:]),
                       waits=[ev_d3], inc="vr")
        ev_y1 = P.op("vector", lambda e: e.tensor_scalar(
            s_sb[:], s_psel[:], -float(ZROW), None, ALU.add),
            waits=[ev_dest, ev_ps], inc="vr")
        ev_y2 = P.op("vector", lambda e: e.tensor_tensor(
            s_sb[:], s_sb[:], s_valid[:], ALU.mult), waits=[ev_y1], inc="vr")
        ev_y3 = P.op("vector", lambda e: e.tensor_scalar(
            s_sb[:], s_sb[:], float(ZROW), None, ALU.add), waits=[ev_y2], inc="vr")
        ev_c1 = P.op("vector", lambda e: e.tensor_scalar(
            s_sa[:], s_sb[:], 1024.0, None, ALU.is_ge),
            waits=[ev_y3, ev_dest], inc="vr")
        ev_c2 = P.op("vector", lambda e: e.tensor_scalar(
            s_valid[:], s_sb[:], 2048.0, None, ALU.is_ge),
            waits=[ev_y3, ev_vd, ev_d2], inc="vr")
        ev_c3 = P.op("vector", lambda e: e.tensor_tensor(
            s_sa[:], s_sa[:], s_valid[:], ALU.add), waits=[ev_c1, ev_c2], inc="vr")
        ev_c4 = P.op("vector", lambda e: e.tensor_scalar(
            s_valid[:], s_sb[:], 3072.0, None, ALU.is_ge),
            waits=[ev_c3], inc="vr")
        ev_c5 = P.op("vector", lambda e: e.tensor_tensor(
            s_sa[:], s_sa[:], s_valid[:], ALU.add), waits=[ev_c4], inc="vr")
        ev_c6 = P.op("vector", lambda e: e.tensor_scalar(
            s_scr32[:], s_sb[:], 3584.0, None, ALU.is_ge),
            waits=[ev_c5], inc="vr")
        ev_c7a = P.op("vector", lambda e: e.tensor_scalar(
            s_scr32b[:], s_sb[:], 4096.0, None, ALU.is_ge),
            waits=[ev_c5], inc="vr")
        ev_c7b = P.op("vector", lambda e: e.tensor_tensor(
            s_sa[:], s_sa[:], s_scr32[:], ALU.add), waits=[ev_c6], inc="vr")
        ev_c7 = P.op("vector", lambda e: e.tensor_tensor(
            s_sa[:], s_sa[:], s_scr32b[:], ALU.add), waits=[ev_c7a, ev_c7b],
            inc="vr")
        ev_y4 = P.op("vector", lambda e: e.tensor_scalar(
            s_yidxf[:], s_sa[:], 7168.0, None, ALU.mult), waits=[ev_c7], inc="vr")
        ev_y5 = P.op("vector", lambda e: e.scalar_tensor_tensor(
            s_yidxf[:], s_scr32[:], -3584.0, s_yidxf[:], ALU.mult, ALU.add),
            waits=[ev_y4, ev_c6], inc="vr")
        ev_y5x = P.op("vector", lambda e: e.scalar_tensor_tensor(
            s_yidxf[:], s_scr32b[:], -3584.0, s_yidxf[:], ALU.mult, ALU.add),
            waits=[ev_y5, ev_c7a], inc="vr")
        ev_y5b = P.op("vector", lambda e: e.tensor_tensor(
            s_yidxf[:], s_yidxf[:], s_sb[:], ALU.add), waits=[ev_y5x], inc="vr")
        ev_y6 = P.op("vector", lambda e: e.tensor_scalar(
            s_valid[:], s_valid[:], -512.0, 1024.0, ALU.mult, ALU.add),
            waits=[ev_c7], inc="vr")
        ev_y6b = P.op("vector", lambda e: e.scalar_tensor_tensor(
            s_valid[:], s_scr32b[:], -248.0, s_valid[:], ALU.mult, ALU.add),
            waits=[ev_y6, ev_c7a], inc="vr")
        ev_y7 = P.op("vector", lambda e: e.tensor_tensor(
            s_valid[:], s_valid[:], s_esel[:], ALU.mult),
            waits=ev_es + [ev_y6b], inc="vr")
        ev_y8 = P.op("vector", lambda e: e.tensor_tensor(
            s_yidxf[:], s_yidxf[:], s_valid[:], ALU.add),
            waits=[ev_y5b, ev_y7], inc="vr")
        ev_yidx = P.op("vector", lambda e: e.tensor_copy(s_yidx[:], s_yidxf[:]),
                       waits=[ev_y8], inc="vr")
        ev_scat = []
        for j in range(32):
            ev_scat.append(P.op("gpsimd", lambda e, j=j: e.indirect_dma_start(
                out=tbl_in.ap()[:, :],
                out_offset=IndirectOffsetOnAxis(s_dest[:, j:j + 1], axis=0),
                in_=s_payl[:, j * 4:(j + 1) * 4], in_offset=None),
                waits=[ev_dest] + ev_pay + ev_tz + ev_yz, inc="gs",
                inck=16, dma=True))
        ev_art = P.op("gpsimd", lambda e: e.collective_compute(
            "AllReduce", ALU.add, replica_groups=[list(range(8))],
            ins=[tbl_in.ap().opt()], outs=[tbl_out.ap().opt()]),
            waits=ev_scat, inc="cct", dma=True)

        if stop_after == "routing":
            _finish(P, DBG, debug_taps, SBT, y_out, [ev_art, ev_agkv])
            _emit(P, nc, es)
            return nc

        kvflat = kv_all.ap().rearrange("(a b) -> a b", b=1)
        ev_kld = []
        for kb in range(8):
            evlo = P.op("gpsimd", lambda e, kb=kb: e.indirect_dma_start(
                out=s_kall[0:64, kb * 512:(kb + 1) * 512],
                out_offset=None, in_=kvflat,
                in_offset=IndirectOffsetOnAxis(s_kidx[0:64, kb:kb + 1], axis=0)),
                waits=[ev_agkv, EV["kidx"]], inc="gk", inck=16, dma=True)
            ev_kld.append(P.op("gpsimd", lambda e, kb=kb: e.dma_start(
                out=s_kall[64:128, kb * 512:(kb + 1) * 512],
                in_=s_kall[0:64, kb * 512:(kb + 1) * 512]),
                waits=[evlo], inc="gk", inck=16, dma=True))
        ev_vld = []
        for i in range(32):
            ev_vld.append(P.op("gpsimd", lambda e, i=i: e.indirect_dma_start(
                out=s_vall[:, i * 65:(i + 1) * 65], out_offset=None,
                in_=kvflat,
                in_offset=IndirectOffsetOnAxis(s_vidx[:, i:i + 1], axis=0)),
                waits=[ev_agkv, EV["vidx"]], inc="gv", inck=16, dma=True))

        ev_kld_total = ev_kld[-1]
        ev_vld_total = ev_vld[-1]
        pair = 0
        ev_stt = [None] * NPAIR
        ev_exp = [None] * NPAIR
        ev_av = [None] * NPAIR
        ev_zev_bb = [None] * 4
        ev_rec_tt = [None] * 16
        prev_rec = None
        prev_zev = None
        SCB = [0, 2, 3, 4]
        ev_qb_all = ev_qb[-1]

        def emit_av(p, i0, n_k):
            i = i0
            ev_av[p] = P.op("tensor", lambda e, i=i, p=p,
                            st=(i == 0), sp=(i == n_k - 1): e.matmul(
                pb[1][0:65, 0:512], s_vall[:, i * 65:(i + 1) * 65],
                s_pt[p % 4][:], start=st, stop=sp),
                waits=[ev_exp[p], ev_vld_total,
                       prev_zev if i == 0 else None], inc="mm")

        for jj in range(4):
            bb = 3 - jj
            qoff = bb * 512
            n_k = ATT_BOUNDS[jj]
            base = pair
            for ii in range(n_k // 2):
                for half in range(2):
                    p = pair
                    i = 2 * ii + half
                    bank = pb[SCB[p % 4]]
                    kt = s_kall[half * 64:(half + 1) * 64,
                                i * 128:(i + 1) * 128]
                    qt = s_q[half * 64:(half + 1) * 64, qoff:qoff + 512]
                    evsc = P.op("tensor", lambda e, kt=kt, qt=qt, bank=bank,
                                tp=(half * 64, 0): e.matmul(
                        bank[:, 0:512], kt, qt, start=True, stop=True,
                        tile_position=tp),
                        waits=[ev_kld_total, ev_qev_all[bb], ev_qb_all,
                               ev_stt[p - 4] if p >= 4 else None,
                               ev_qk_free if p <= 3 else None,
                               ev_vg_free if p <= 3 else None,
                               ev_logit[15] if p <= 3 else None,
                               prev_fm.get(p % 2) if p <= 3 else None],
                        inc="mm")
                    evmk = P.op("vector", lambda e, p=p: e.tensor_scalar(
                        s_mask[:], s_dmat[:], s_dtbl[:, p:p + 1], None,
                        ALU.is_gt),
                        waits=[EV["dmat"], EV["dtbl"],
                               ev_stt[p - 1] if p >= 1 else None], inc="vr")
                    ev_stt[p] = P.op("vector", lambda e, p=p, bank=bank:
                                     e.scalar_tensor_tensor(
                        s_u[p % 4][:], s_mask[:], MASKV, bank[:, 0:512],
                        ALU.mult, ALU.add),
                        waits=[evmk, evsc,
                               ev_exp[p - 4] if p >= 4 else None], inc="vr")
                    ev_exp[p] = P.op("scalar", lambda e, p=p: e.activation(
                        out=s_pt[p % 4][:], in_=s_u[p % 4][:], func=AF.Exp,
                        bias=0.0, scale=0.125),
                        waits=[ev_stt[p], ev_c125,
                               ev_av[p - 4] if p >= 4 else None], inc="sr")
                    pair += 1
                if ii >= 1:
                    emit_av(base + 2 * (ii - 1), 2 * (ii - 1), n_k)
                    emit_av(base + 2 * (ii - 1) + 1, 2 * (ii - 1) + 1, n_k)
            emit_av(base + n_k - 2, n_k - 2, n_k)
            emit_av(base + n_k - 1, n_k - 1, n_k)
            ev_zev_bb[bb] = P.op("vector", lambda e, qoff=qoff: e.tensor_copy(
                s_z[:, qoff:qoff + 512], pb[1][0:65, 0:512]),
                waits=[ev_av[pair - 1]], inc="vr")
            prev_zev = ev_zev_bb[bb]
            for ti in range(4):
                tt = bb * 4 + ti
                evdt = P.op("tensor", lambda e, qoff=qoff, ti=ti: e.matmul(
                    pb[7][:, 0:1],
                    s_z[64:65, qoff + ti * 128:qoff + (ti + 1) * 128],
                    s_onecol[64:65, 0:1], start=True, stop=True),
                    waits=[ev_zev_bb[bb], ev_oc, prev_rec], inc="mm")
                prev_rec = P.op("vector", lambda e, tt=tt: e.reciprocal(
                    s_recip[:, tt:tt + 1], pb[7][:, 0:1]),
                    waits=[evdt], inc="vr")
                ev_rec_tt[tt] = prev_rec

        if stop_after == "attention":
            _finish(P, DBG, debug_taps, SBT, y_out,
                    [ev_art] + ev_rec_tt + list(ev_zev_bb))
            _emit(P, nc, es)
            return nc

        ev_tg = []
        for j in range(NST):
            ev_tg.append(P.op("gpsimd", lambda e, j=j: e.indirect_dma_start(
                out=s_trow[:, j * 4:(j + 1) * 4], out_offset=None,
                in_=tbl_out.ap()[:, :],
                in_offset=IndirectOffsetOnAxis(s_tslice[:, j:j + 1], axis=0)),
                waits=[ev_art, EV["tslice"]], inc="gt", inck=16, dma=True))
        ev_tc = P.op("vector", lambda e: e.tensor_copy(
            s_tcast[:], rv(s_trow, "p (j f) -> p j f", f=4)[:, :, 0:1]),
            waits=[ev_tg[-1]], inc="vr")

        prev_fm[("x", 0)] = ev_stt[NPAIR - 1]
        prev_fm[("x", 1)] = ev_zev_bb[0]
        prev_fm[("x", 6)] = ev_bglbv
        ev_lnx = [None] * NST
        ev_xgf_done = [None] * NST
        ev_gelu_hist = []
        ev_hsplit_hist = []
        ev_gel_rd = [None]
        ev_w1_user = {}
        ev_w2_hist = []
        ev_ydma = []
        ev_ydma_sg = {}
        ev_ccy = []
        ev_yev_used = [None, None]
        prev_sg_y = [None, None, None, ev_rec_tt[0]]
        prev_sg_y[3] = ev_rec_tt[3]
        sgbase = 0
        for sgi, w in enumerate(SGW):
            ntl = w // 128
            j0 = sgbase // 128
            for ti in range(ntl):
                j = j0 + ti
                evg = P.op("gpsimd", lambda e, j=j: e.indirect_dma_start(
                    out=s_xg[j % 4][:], out_offset=None,
                    in_=DI["x_full"][:, :],
                    in_offset=IndirectOffsetOnAxis(s_tcast[:, j:j + 1], axis=0)),
                    waits=[ev_tc, ev_lnx[j - 4] if j >= 4 else None],
                    inc=f"gx{j % 4}", inck=16, dma=True)
                ev_lnx[j] = P.op("vector", lambda e, j=j: e.tensor_scalar(
                    s_xng[:, (j % 4) * D:(j % 4 + 1) * D], s_xg[j % 4][:],
                    s_trow[:, j * 4 + 2:j * 4 + 3],
                    s_trow[:, j * 4 + 3:j * 4 + 4], ALU.mult, ALU.add),
                    waits=[evg, ev_tg[-1],
                           ev_xgf_done[j - 4] if j >= 4 else ev_tile_tp[12 + (j % 4)]],
                    inc="vr")
            evfm = []
            TRB = (0, 1, 6)
            for ti in range(ntl):
                j = j0 + ti
                for dcc in range(8):
                    bank = TRB[(ti * 8 + dcc) % 3]
                    evt = P.op("tensor", lambda e, j=j, dcc=dcc, bank=bank:
                               e.transpose(
                        pb[bank][:, 0:128],
                        s_xng[:, (j % 4) * D + dcc * 128:(j % 4) * D + (dcc + 1) * 128],
                        s_id[:]),
                        waits=[ev_lnx[j], prev_fm.get(("x", bank))], inc="tp")
                    evhh = P.op(
                        "vector", lambda e, ti=ti, dcc=dcc, bank=bank: e.tensor_copy(
                            s_xgfh[:, dcc * 512 + ti * 128: dcc * 512 + (ti + 1) * 128],
                            pb[bank][:, 0:128]), waits=[evt], inc="fm")
                    prev_fm[("x", bank)] = P.op(
                        "vector", lambda e, ti=ti, dcc=dcc, bank=bank: e.tensor_tensor(
                            s_xgfl[:, dcc * 512 + ti * 128: dcc * 512 + (ti + 1) * 128],
                            pb[bank][:, 0:128],
                            s_xgfh[:, dcc * 512 + ti * 128: dcc * 512 + (ti + 1) * 128],
                            ALU.subtract), waits=[evhh], inc="fm")
                    evfm.append(prev_fm[("x", bank)])
                ev_xgf_done[j] = evfm[-1]
            for ti in range(ntl):
                ev_xgf_done[j0 + ti] = evfm[-1]
            ev_w1g = {}

            w1v = w1s.ap().rearrange("p (d g h c) -> p d g h c",
                                     d=8, g=16, h=2, c=256)

            def load_w1(gq):
                gqg = sgi * 16 + gq
                spl = [SPLIT_EV[(0, dcc * 128, (gq * 256) // 512 * 512)]
                       for dcc in range(8)]
                ev_w1g[gq] = P.op("sync", lambda e, gq=gq: e.dma_start(
                    out=s_w1hl[:, (gq % 2) * 4096:(gq % 2 + 1) * 4096],
                    in_=w1v[:, :, gq:gq + 1, :, :]),
                    waits=[ev_w1_user.get(gqg - 2)] + spl,
                    inc="w1a" if gq % 2 == 0 else "w1b", inck=16, dma=True)

            load_w1(0)
            load_w1(1)
            ev_hmm_last = {}
            for hc in range(32):
                gq = hc // 2
                hb = 2 + (hc % 2)
                hgi = sgi * 32 + hc
                lastm = None
                np3 = 0
                for hilo, xs in ((0, s_xgfh), (0, s_xgfl), (1, s_xgfh)):
                    for dcc in range(8):
                        woff = ((gq % 2) * 4096 + dcc * 512 + hilo * 256
                                + (hc % 2) * 128)
                        lastm = P.op("tensor", lambda e, hc=hc, dcc=dcc, hb=hb,
                                     w=w, woff=woff, xs=xs, np3=np3: e.matmul(
                            pb[hb][:, 0:w],
                            s_w1hl[:, woff:woff + 128],
                            xs[:, dcc * 512:dcc * 512 + w],
                            start=(np3 == 0 and dcc == 0),
                            stop=(np3 == 2 and dcc == 7)),
                            waits=([ev_w1g[gq], evfm[-1],
                                    ev_gelu_hist[hgi - 2] if hgi >= 2 else None,
                                    ev_qk_free if hgi == 0 else None,
                                    ev_stt[NPAIR - 1] if hgi <= 1 else None]
                                   if (np3 == 0 and dcc == 0) else []),
                            inc="mm" if (np3 == 2 and dcc == 7) else None)
                    np3 += 1
                ev_hmm_last[hc] = lastm
                arena_guard = ([ev_av[NPAIR - 1], ev_stt[NPAIR - 1],
                                ev_kdma[-1], ev_vdma[-1]]
                               if (sgi == 0 and hc == 0) else [])
                evgel = P.op("scalar", lambda e, hc=hc, hb=hb, w=w:
                             e.activation(
                    out=s_gel[:, 0:w], in_=pb[hb][:, 0:w],
                    func=AF.Gelu_apprx_tanh, bias=s_b1c[:, hc:hc + 1], scale=1.0),
                    waits=[lastm, EV["b1c"], ev_gel_rd[0]] + arena_guard,
                    inc="sr")
                ev_gelu_hist.append(evgel)
                evhh2 = P.op("vector", lambda e, hc=hc, w=w: e.tensor_copy(
                    s_hh[:, hc * 512:hc * 512 + w], s_gel[:, 0:w]),
                    waits=[evgel], inc="fm")
                ev_gel_rd[0] = P.op("vector", lambda e, hc=hc, w=w:
                                    e.tensor_tensor(
                    s_hl[:, hc * 512:hc * 512 + w], s_gel[:, 0:w],
                    s_hh[:, hc * 512:hc * 512 + w], ALU.subtract),
                    waits=[evhh2], inc="fm")
                ev_hsplit_hist.append(ev_gel_rd[0])
                if sgi == 0:
                    emit_split(2)
                if hc % 2 == 1:
                    ev_w1_user[sgi * 16 + hc // 2] = lastm
                    if hc // 2 + 2 < 16:
                        load_w1(hc // 2 + 2)
            w2v = w2s.ap().rearrange("p (t g hl) -> p t g hl", t=32, g=2)
            for dg in range(2):
                for hc in range(32):
                    w2i = len(ev_w2_hist)
                    spl_ev = SPLIT_EV[(1, hc * 128, dg * 512)]
                    evw2 = P.op("sync", lambda e, hc=hc, dg=dg: e.dma_start(
                        out=s_w2hl[:, (hc % 2) * 1024:(hc % 2 + 1) * 1024],
                        in_=w2v[:, hc:hc + 1, dg:dg + 1, :]),
                        waits=([ev_w2_hist[w2i - 2][1]] if w2i >= 2 else []
                               ) + [spl_ev],
                        inc="w2a" if hc % 2 == 0 else "w2b", inck=16, dma=True)
                    lastm = None
                    np3 = 0
                    for hs, hilo in ((s_hh, 0), (s_hh, 1), (s_hl, 0)):
                        for st in range(ntl):
                            lastm = P.op("tensor", lambda e, hc=hc, st=st,
                                         hs=hs, hilo=hilo, np3=np3: e.matmul(
                                pb[4 + st][:, 0:512],
                                hs[:, hc * 512 + st * 128:hc * 512 + (st + 1) * 128],
                                s_w2hl[:, (hc % 2) * 1024 + hilo * 512:
                                       (hc % 2) * 1024 + hilo * 512 + 512],
                                start=(hc == 0 and np3 == 0),
                                stop=(hc == 31 and np3 == 2)),
                                waits=([ev_hsplit_hist[sgi * 32 + hc], evw2] +
                                       ([prev_sg_y[st]]
                                        if (hc == 0 and np3 == 0) else [])),
                                inc="mm" if (st == ntl - 1 and np3 == 2) else None)
                        np3 += 1
                    ev_w2_hist.append((evw2, lastm))
                for st in range(ntl):
                    eve = P.op("vector", lambda e, st=st, dg=dg: e.tensor_tensor(
                        s_yev[st % 2][:], pb[4 + st][:, 0:512],
                        s_b2bc[:, dg * 512:(dg + 1) * 512], ALU.add),
                        waits=[lastm, ev_yev_used[st % 2], ev_bcast_done],
                        inc="vr")
                    prev_sg_y[st] = eve
                    evd = P.op("sync", lambda e, st=st, dg=dg,
                               r0=sgbase + st * 128: e.dma_start(
                        out=y_in.ap()[r0:r0 + 128, dg * 512:(dg + 1) * 512],
                        in_=s_yev[st % 2][:]),
                        waits=[eve], inc="yya" if st % 2 == 0 else "yyb",
                        inck=16, dma=True)
                    ev_yev_used[st % 2] = evd
                    ev_ydma.append(evd)
                    ev_ydma_sg.setdefault(sgi, []).append(evd)
            if sgi in (1, 3, 5):
                c = (sgi - 1) // 2
                ev_ccy.append(P.op("gpsimd", lambda e, c=c: e.collective_compute(
                    "AllGather", ALU.bypass, replica_groups=[list(range(8))],
                    ins=[y_in.ap()[1024 * c:1024 * (c + 1), :].opt()],
                    outs=[y_all.ap()[8192 * c:8192 * (c + 1), :].opt()]),
                    waits=ev_ydma_sg[sgi - 1] + ev_ydma_sg[sgi],
                    inc=f"ccy{c}", dma=True))
            if sgi == 6:
                ev_ccy.append(P.op("gpsimd", lambda e: e.collective_compute(
                    "AllGather", ALU.bypass, replica_groups=[list(range(8))],
                    ins=[y_in.ap()[3072:3584, :].opt()],
                    outs=[y_all.ap()[24576:28672, :].opt()]),
                    waits=ev_ydma_sg[6], inc="ccy3", dma=True))
            if sgi == 7:
                ev_ccy.append(P.op("gpsimd", lambda e: e.collective_compute(
                    "AllGather", ALU.bypass, replica_groups=[list(range(8))],
                    ins=[y_in.ap()[3584:4096, :].opt()],
                    outs=[y_all.ap()[28672:32768, :].opt()]),
                    waits=ev_ydma_sg[7], inc="ccy4", dma=True))
            if sgi == 8:
                ev_ccy.append(P.op("gpsimd", lambda e: e.collective_compute(
                    "AllGather", ALU.bypass, replica_groups=[list(range(8))],
                    ins=[y_in.ap()[4096:YROWS, :].opt()],
                    outs=[y_all.ap()[32768:8 * YROWS, :].opt()]),
                    waits=ev_ydma_sg[8] + ev_yz, inc="ccy5", dma=True))
            sgbase += w

        if stop_after == "expert":
            _finish(P, DBG, debug_taps, SBT, y_out, list(ev_ccy))
            _emit(P, nc, es)
            return nc

        ev_pst = [None] * 16
        ev_xtu = None
        ev_xbu = None
        ev_xl_prev = [ev_xload_prev[0], ev_xload_prev[1]]
        ev_zo_read = None
        ev_zo_free = [prev_fm[("x", 0)], prev_fm[("x", 1)]]
        for tt in range(16):
            evx = P.op("scalar", lambda e, tt=tt: e.dma_start(
                out=s_xt[tt % 2][:], in_=DI["x_own"][tt * 128:(tt + 1) * 128, :]),
                waits=[ev_xtu, ev_xn[14 + (tt % 2)] if tt < 2 else None,
                       ev_xl_prev[tt % 2]],
                inc="dmaA" if tt % 2 == 0 else "dmaB", inck=16, dma=True)
            ev_xl_prev[tt % 2] = evx
            evxb = P.op("gpsimd", lambda e, tt=tt: e.tensor_tensor(
                s_xb[:], s_xt[tt % 2][:], s_bobc[:], ALU.add),
                waits=[evx, ev_bcast_done, ev_xbu], inc="gc")
            ev_zoev = []
            for dg in range(2):
                evzm = P.op("tensor", lambda e, tt=tt, dg=dg: e.matmul(
                    pb[dg][:, 0:512], s_z[0:64, tt * 128:(tt + 1) * 128],
                    s_wo[:, dg * 512:(dg + 1) * 512], start=True, stop=True),
                    waits=[ev_zev_bb[tt // 4], EV["wo"], ev_zo_free[dg]],
                    inc="mm")
                ev_zoev.append(P.op("scalar", lambda e, tt=tt, dg=dg: e.activation(
                    out=s_zo[:, dg * 512:(dg + 1) * 512], in_=pb[dg][:, 0:512],
                    func=AF.Identity, bias=0.0, scale=s_recip[:, tt:tt + 1]),
                    waits=[evzm, ev_rec_tt[tt], ev_zo_read], inc="sr"))
                ev_zo_free[dg] = ev_zoev[-1]
            evp = P.op("vector", lambda e: e.tensor_tensor(
                s_out[:], s_zo[:], s_xb[:], ALU.add),
                waits=ev_zoev + [evxb,
                                 ev_pst[tt - 1] if tt >= 1 else None],
                inc="vr")
            ev_zo_read = evp
            ev_xbu = evp
            ev_xtu = evp
            ev_pst[tt] = P.op("scalar", lambda e, tt=tt: e.dma_start(
                out=pscr.ap()[tt * 128:(tt + 1) * 128, :], in_=s_out[:]),
                waits=[evp], inc="yya" if tt % 2 == 0 else "yyb",
                inck=16, dma=True)

        ev_outd = []
        ev_g0u = [None, None]
        ev_g1u = [None, None]
        ev_g0ev = [None, None]
        ev_g1ev = [None, None]
        ev_plev = [None, None]
        ev_pbu = None
        ev_accu = None
        for tt in range(16):
            c4, t4 = divmod(tt, 4)
            f0 = c4 * 8 + t4 * 2
            evy0 = P.op("gpsimd", lambda e, f0=f0, tt=tt: e.indirect_dma_start(
                out=s_yg0[tt % 2][:], out_offset=None, in_=y_all.ap()[:, :],
                in_offset=IndirectOffsetOnAxis(s_yidx[:, f0:f0 + 1], axis=0)),
                waits=ev_ccy + [ev_yidx, ev_g0u[tt % 2], ev_g0ev[tt % 2]],
                inc="gy0a" if tt % 2 == 0 else "gy0b", inck=16, dma=True)
            ev_g0ev[tt % 2] = evy0
            evy1 = P.op("gpsimd", lambda e, f0=f0, tt=tt: e.indirect_dma_start(
                out=s_yg1[tt % 2][:], out_offset=None, in_=y_all.ap()[:, :],
                in_offset=IndirectOffsetOnAxis(s_yidx[:, f0 + 1:f0 + 2], axis=0)),
                waits=ev_ccy + [ev_yidx, ev_g1u[tt % 2], ev_g1ev[tt % 2]],
                inc="gy1a" if tt % 2 == 0 else "gy1b", inck=16, dma=True)
            ev_g1ev[tt % 2] = evy1
            evpl = P.op("sync", lambda e, tt=tt: e.dma_start(
                out=s_xb[:], in_=pscr.ap()[tt * 128:(tt + 1) * 128, :]),
                waits=ev_ccy + [ev_pst[15], ev_pbu, ev_plev[tt % 2]],
                inc="dmaA" if tt % 2 == 0 else "dmaB", inck=16, dma=True)
            ev_plev[tt % 2] = evpl
            eva = P.op("vector", lambda e, tt=tt: e.tensor_scalar(
                s_acc[:], s_yg0[tt % 2][:], s_g1[:, tt:tt + 1], None, ALU.mult),
                waits=[evy0, ev_top[tt][2], ev_accu], inc="vr")
            evb = P.op("vector", lambda e, tt=tt: e.scalar_tensor_tensor(
                s_acc[:], s_yg1[tt % 2][:], s_g2[:, tt:tt + 1], s_acc[:],
                ALU.mult, ALU.add),
                waits=[eva, evy1, ev_top[tt][3]], inc="vr")
            ev_g0u[tt % 2] = evb
            ev_g1u[tt % 2] = evb
            evd2 = P.op("vector", lambda e: e.tensor_tensor(
                s_out[:], s_acc[:], s_xb[:], ALU.add),
                waits=[evb, evpl,
                       ev_outd[-1] if ev_outd else ev_pst[15]], inc="vr")
            ev_pbu = evd2
            ev_accu = evd2
            ev_outd.append(P.op("sync", lambda e, tt=tt: e.dma_start(
                out=y_out[tt * 128:(tt + 1) * 128, :], in_=s_out[:]),
                waits=[evd2], inc="oo", inck=16, dma=True))

        _finish(P, DBG, debug_taps, SBT, y_out, ev_outd)
        _emit(P, nc, es)
    return nc


def _finish(P, DBG, debug_taps, SBT, y_out, final_evs):
    evs = [e for e in final_evs if e is not None]
    for (dname, sname, shape) in debug_taps:
        tile = SBT[sname]
        evs.append(P.op("gpsimd", lambda e, d=DBG[dname], t=tile, s=shape:
                        e.dma_start(out=d[:, :], in_=t[0:s[0], 0:s[1]]),
                        waits=list(evs), inc="gz2", inck=16, dma=True))
    if not any(e.sem == "oo" for e in evs):
        z = SBT["s_zero"]
        for ttt in range(16):
            for hf in range(2):
                evs.append(P.op("gpsimd", lambda e, ttt=ttt, hf=hf: e.dma_start(
                    out=y_out[ttt * 128:(ttt + 1) * 128, hf * 512:(hf + 1) * 512],
                    in_=z[:, 0:512]), inc="gz2", inck=16, dma=True))
    P.op("gpsimd", lambda e: e.nop(), waits=evs)


def _emit(P, nc, es):
    blk = es.enter_context(nc.Block())
    semh = {}
    for nm in P.sem_names:
        semh[nm] = es.enter_context(nc.semaphore(nm))

    def emit_for(engname):
        def body(eng):
            for item in P.items[engname]:
                if item[0] == "wait":
                    eng.wait_ge(semh[item[1]], item[2])
                elif item[0] == "drain":
                    eng.drain()
                else:
                    instr = item[1](eng)
                    for (sm, k) in item[2]:
                        instr.then_inc(semh[sm], k)
        return body

    for engname in ENGS:
        getattr(blk, engname)(emit_for(engname))


def kernel(**inputs):
    in_maps = _shard_inputs(inputs)
    nc = build_program()
    res = run_bass_kernel_spmd(nc, in_maps, core_ids=list(range(NCORE)))
    return _unshard_output(res.results)
